# version 21
# baseline (speedup 1.0000x reference)
import contextlib
import numpy as np
import ml_dtypes
import concourse.bass as bass
import concourse.mybir as mybir
from concourse.bass_utils import run_bass_kernel_spmd

F32 = mybir.dt.float32
BF16 = mybir.dt.bfloat16
I32 = mybir.dt.int32
AF = mybir.ActivationFunctionType
OP = mybir.AluOpType

NCORE = 8
TOK = 2048
D = 1024
CH = 8
TQ = 512
NTQ = 4
NW = 6
EPS = 1e-5
BF = ml_dtypes.bfloat16

UNITS = [("W", 0), ("W", 1), ("A", 0), ("A", 1), ("B", 0, 0), ("W", 2),
         ("A", 2), ("B", 1, 0), ("W", 3), ("A", 3), ("B", 2, 0), ("W", 4),
         ("B", 3, 0), ("W", 5),
         ("P", 0, 1, 1), ("P", 2, 3, 1), ("P", 0, 1, 2), ("P", 2, 3, 2)]


def _build_tables():
    ginfo = {}
    inst_of = {}
    gelu_pre = {}
    zinfo = {}
    second = {}
    order = []
    ninst = 0
    nb = 0
    nz = 0
    for u in UNITS:
        if u[0] == "W":
            continue
        if u[0] == "A":
            tq = u[1]
            inst_of[u] = ninst
            gelu_pre[ninst] = nb
            ninst += 1
            for mc in range(8):
                order.append(("h", tq, mc))
        elif u[0] == "B":
            _, it, b = u
            inst_of[u] = ninst
            gelu_pre[ninst] = nb
            ninst += 1
            nb += 1
            zinfo[(it, b, 0)] = nz
            nz += 1
            for mc in range(8):
                order.append(("p", it, b, 0, mc))
            zinfo[(it, b, 1)] = nz
            nz += 1
            for mc in range(8):
                order.append(("p", it, b, 1, mc))
        else:
            _, itX, itY, b = u
            for idx, it in enumerate((itX, itY)):
                inst_of[("B", it, b)] = ninst
                gelu_pre[ninst] = nb
                if idx == 1:
                    second[ninst] = True
                ninst += 1
                nb += 1
            for it in (itX, itY):
                zinfo[(it, b, 0)] = nz
                nz += 1
            for it in (itX, itY):
                for mc in range(8):
                    order.append(("p", it, b, 0, mc))
            for it in (itX, itY):
                zinfo[(it, b, 1)] = nz
                nz += 1
            for it in (itX, itY):
                for mc in range(8):
                    order.append(("p", it, b, 1, mc))
            if b == 2:
                for it in (itX, itY):
                    for mc in range(8):
                        order.append(("s", it, mc))
    ginfo_banks = [0, 0, 0, 0]
    for pos, key in enumerate(order):
        bank = pos % 4
        ginfo[key] = (pos, bank, ginfo_banks[bank])
        ginfo_banks[bank] += 1
    return ginfo, inst_of, gelu_pre, zinfo, second


_GINFO, _INST, _GPRE, _ZINFO, _SECOND = _build_tables()
_ZREV = {v: k for k, v in _ZINFO.items()}


def _inst_a(tq):
    return _INST[("A", tq)]


def _inst_b(it, b):
    return _INST[("B", it, b)]


def _stream_j0(it, p, mc):
    return it * 16 + p * 8 + mc


def _stream_key0(j):
    return ("p", j // 16, 0, (j % 16) // 8, j % 8)


def _ar_cnt(w, mc):
    return (w - 2) * 4 + mc // 2


def build_nc():
    nc = bass.Bass("TRN2", target_bir_lowering=False, detect_race_conditions=False)

    x_fm = nc.dram_tensor("x_fm", [D, TOK], F32, kind="ExternalInput")
    q_t = nc.dram_tensor("q_t", [NW, D, D], I32, kind="ExternalInput")
    sc_t = nc.dram_tensor("sc_t", [NW, 64, D], F32, kind="ExternalInput")
    lat_h = nc.dram_tensor("lat_h", [NW, D, 32], F32, kind="ExternalInput")
    lbt_h = nc.dram_tensor("lbt_h", [32, NW * D], F32, kind="ExternalInput")
    bias_h = nc.dram_tensor("bias_h", [NW, 128, 8], F32, kind="ExternalInput")
    lng_h = nc.dram_tensor("lng_h", [3, 128, 8], F32, kind="ExternalInput")
    lnb_h = nc.dram_tensor("lnb_h", [3, 128, 8], F32, kind="ExternalInput")
    va_h = nc.dram_tensor("va_h", [128, 8], F32, kind="ExternalInput")
    vb_h = nc.dram_tensor("vb_h", [128, 8], F32, kind="ExternalInput")
    vbin_h = nc.dram_tensor("vbin_h", [128, 8], F32, kind="ExternalInput")
    vbout_h = nc.dram_tensor("vbout_h", [128, 8], F32, kind="ExternalInput")
    wint_h = nc.dram_tensor("wint_h", [D, D], F32, kind="ExternalInput")
    woutt_h = nc.dram_tensor("woutt_h", [D, D], F32, kind="ExternalInput")
    m1_h = nc.dram_tensor("m1_h", [128, 64], F32, kind="ExternalInput")
    mp_h = nc.dram_tensor("mp_h", [128, 64], F32, kind="ExternalInput")
    mp2_h = nc.dram_tensor("mp2_h", [128, 64], F32, kind="ExternalInput")
    out_fm = nc.dram_tensor("out_fm", [D, TOK], F32, kind="ExternalOutput")

    w_hbm = nc.dram_tensor("w_hbm", [NW * 8 * 128 * 8 * 128], BF16)
    f_bnc = nc.dram_tensor("f_bnc", [128, 8], F32)
    g_bnc = nc.dram_tensor("g_bnc", [128 * NCORE, 8], F32)

    no33 = np.zeros((128, 33), dtype=BF)
    no33[:, 0] = -1.0 / D
    po33 = np.zeros((128, 33), dtype=BF)
    po33[:, 32] = 1.0 / D
    no33_c = nc.inline_tensor(no33, "no33c")
    po33_c = nc.inline_tensor(po33, "po33c")
    bcones = nc.inline_tensor(np.ones((1, 128), dtype=BF), "bconesc")

    es = contextlib.ExitStack()
    with es:
        sb = lambda name, shape, dt: es.enter_context(nc.sbuf_tensor(name, shape, dt))
        psb = lambda name, shape: es.enter_context(nc.psum_tensor(name, shape, F32))
        sem = lambda name: es.enter_context(nc.semaphore(name))

        xnbf = sb("xnbf", [128, CH * TOK], BF16)
        bhbf = sb("bhbf", [128, CH * TOK], BF16)
        Sst = sb("Sst", [128, NTQ * CH * TQ], BF16)
        wio = sb("wio", [128, CH * D], BF16)
        wstr = sb("wstr", [128, 4 * 1024], BF16)
        qbf = sb("qbf", [128, 2 * 1024], BF16)
        qcbf = sb("qcbf", [128, 2 * 1024], BF16)
        esbf = sb("esbf", [128, 2 * 1024], BF16)
        wchk = sb("wchk", [128, 2 * 1024], BF16)
        latall = sb("latall", [128, NW * 256], BF16)
        lbt_sb = sb("lbt_sb", [32, NW * D], BF16)
        zbf = sb("zbf", [32, TQ], BF16)
        bias_sb = sb("bias_sb", [128, NW * 8], F32)
        lng_sb = sb("lng_sb", [128, 24], F32)
        lnb_sb = sb("lnb_sb", [128, 24], F32)
        va_sb = sb("va_sb", [128, 8], F32)
        vb_sb = sb("vb_sb", [128, 8], F32)
        vbin_sb = sb("vbin_sb", [128, 8], F32)
        vbout_sb = sb("vbout_sb", [128, 8], F32)
        bprod_sb = sb("bprod_sb", [128, 8], F32)
        A_sb = sb("A_sb", [128, 8], F32)
        a512 = sb("a512", [128, 8], F32)
        a1024 = sb("a1024", [128, 8], F32)
        a1536 = sb("a1536", [128, 8], F32)
        p2048 = sb("p2048", [128, 8], F32)
        p4096 = sb("p4096", [128, 8], F32)
        ptmp = sb("ptmp", [128, 8], F32)
        ptmp2 = sb("ptmp2", [128, 8], F32)
        carry = sb("carry", [128, 8], F32)
        carry2 = sb("carry2", [128, 8], F32)
        coef = sb("coef", [128, 64], F32)
        crep = sb("crep", [128, 64], F32)
        m1_sb = sb("m1_sb", [128, 64], F32)
        mp_sb = sb("mp_sb", [128, 64], F32)
        mp2_sb = sb("mp2_sb", [128, 64], F32)
        gg_sb = sb("gg_sb", [128, 64], F32)
        u_sb = sb("u_sb", [128, 64], F32)
        stA = sb("stA", [128, 32], F32)
        stC = sb("stC", [128, 32], F32)
        no_sb = sb("no_sb", [128, 33], BF16)
        po_sb = sb("po_sb", [128, 33], BF16)
        bco_sb = sb("bco_sb", [1, 128], BF16)
        eps_sb = sb("eps_sb", [1, 1], F32)
        neg1_sb = sb("neg1_sb", [128, 1], F32)
        nmu = sb("nmu", [1, TQ], F32)
        msq = sb("msq", [1, TQ], F32)
        mu2 = sb("mu2", [1, TQ], F32)
        rstd = sb("rstd", [1, TQ], BF16)
        nmr = sb("nmr", [1, TQ], BF16)
        c1bf = sb("c1bf", [128, TQ], BF16)
        c0bf = sb("c0bf", [128, TQ], BF16)
        xbf = sb("xbf", [128, CH * TQ], BF16)
        sqs = sb("sqs", [128, 2 * TQ], BF16)
        y1bf = sb("y1bf", [128, CH * TQ], BF16)
        t1bf = sb("t1bf", [128, TQ], BF16)
        t2bf = sb("t2bf", [128, 4 * TQ], BF16)
        ybf = sb("ybf", [128, CH * TQ], BF16)
        gbf = sb("gbf", [128, 2 * TQ], BF16)
        tpbf = sb("tpbf", [128, TQ], BF16)
        x2st = sb("x2st", [128, 2 * TQ], F32)
        outt = sb("outt", [128, 2 * TQ], F32)

        ps_a = psb("ps_a", [128, 512])
        ps_b = psb("ps_b", [128, 512])
        ps_e = psb("ps_e", [128, 512])
        ps_f = psb("ps_f", [128, 512])
        ps_c = psb("ps_c", [128, 512])
        ps_d = psb("ps_d", [128, 512])
        ps_st = psb("ps_st", [33, 512])
        ps_z = psb("ps_z", [32, 512])
        ps_p4 = [ps_a, ps_b, ps_e, ps_f]

        s_ld = sem("s_ld")
        s_lg = sem("s_lg")
        s_lx = [sem("s_lx0"), sem("s_lx1")]
        s_q = sem("s_q")
        s_wl = [sem("s_wl0"), sem("s_wl1"), sem("s_wl2"), sem("s_wl3")]
        s_lx2 = [sem("s_lx20"), sem("s_lx21")]
        s_wo = [sem("s_wo0"), sem("s_wo1")]
        s_wol = sem("s_wol")
        s_od = [sem("s_od0"), sem("s_od1")]
        s_qc = sem("s_qc")
        s_es2 = [sem("s_es20"), sem("s_es21")]
        s_p2 = sem("s_p2")
        s_mm = sem("s_mm")
        s_pfb = [sem("s_pfb0"), sem("s_pfb1"), sem("s_pfb2"), sem("s_pfb3")]
        s_sq2 = sem("s_sq2")
        s_stp = sem("s_stp")
        s_r1 = sem("s_r1")
        s_r2 = sem("s_r2")
        s_r3 = sem("s_r3")
        s_r4 = sem("s_r4")
        s_r5 = sem("s_r5")
        s_bc = sem("s_bc")
        s_bce = sem("s_bce")
        s_apl = sem("s_apl")
        s_gel = sem("s_gel")
        s_ar = [sem("s_ar0"), sem("s_ar1")]
        s_bpr = sem("s_bpr")
        s_scn = sem("s_scn")
        s_scn2 = sem("s_scn2")
        s_vch = sem("s_vch")
        s_vch2 = sem("s_vch2")
        s_cc = sem("s_cc")
        s_cc2 = sem("s_cc2")
        s_g = sem("s_g")
        s_csc = sem("s_csc")
        s_sil = sem("s_sil")
        s_pou = sem("s_pou")
        s_woc = sem("s_woc")
        s_zm = sem("s_zm")
        s_ze = sem("s_ze")

        def a_bcast(c, n):
            return bass.AP(A_sb.ap().tensor, c, [[8, 128], [0, n]])

        def lbT_slice(w, mc):
            return bass.AP(lbt_sb.ap().tensor, w * D + mc * 128,
                           [[NW * D, 32], [1, 128]])

        def y1t(inst, c):
            if _SECOND.get(inst):
                t = qcbf if c < 4 else esbf
                return t[:, (c % 4) * TQ:(c % 4) * TQ + TQ]
            return y1bf[:, c * TQ:(c + 1) * TQ]

        with nc.Block() as blk:
            @blk.sync
            def _(sync):
                nld = [0]

                def d(out, in_):
                    sync.dma_start(out=out, in_=in_).then_inc(s_ld, 16)
                    nld[0] += 16
                d(bias_sb[:], bass.AP(bias_h, 0, [[8, 128], [1024, NW], [1, 8]]))
                d(lng_sb[:], bass.AP(lng_h, 0, [[8, 128], [1024, 3], [1, 8]]))
                d(lnb_sb[:], bass.AP(lnb_h, 0, [[8, 128], [1024, 3], [1, 8]]))
                d(va_sb[:], va_h.ap())
                d(vb_sb[:], vb_h.ap())
                d(vbin_sb[:], vbin_h.ap())
                d(vbout_sb[:], vbout_h.ap())
                d(m1_sb[:], m1_h.ap())
                d(mp_sb[:], mp_h.ap())
                d(mp2_sb[:], mp2_h.ap())
                d(no_sb[:], no33_c.ap())
                d(po_sb[:], po33_c.ap())
                d(bco_sb[:], bcones.ap())

                def arena_stream(w, mc):
                    cnt = _ar_cnt(w, mc)
                    if cnt >= 2:
                        sync.wait_ge(s_ar[mc % 2], 16 * (cnt - 1))
                    sync.dma_start(
                        out=xnbf[:, mc * 2048 + (w % 2) * 1024:
                                 mc * 2048 + (w % 2) * 1024 + 1024],
                        in_=bass.AP(w_hbm, w * 1048576 + mc * 131072,
                                    [[1024, 128], [1, 1024]]),
                    ).then_inc(s_ar[mc % 2], 16)

                for u in UNITS:
                    if u[0] == "W":
                        w = u[1]
                        for kc in range(CH):
                            i = w * 8 + kc
                            sync.wait_ge(s_p2, 2 * i + 2)
                            sync.dma_start(
                                out=bass.AP(w_hbm, w * 1048576 + kc * 128,
                                            [[1024, 128], [131072, 8], [1, 128]]),
                                in_=wchk[:, (i % 2) * 1024:(i % 2) * 1024 + 1024],
                            ).then_inc(s_wo[i % 2], 16)
                        if w == 3:
                            pos, bank, seq = _GINFO[("p", 1, 0, 1, 7)]
                            sync.wait_ge(s_pfb[bank], seq + 1)
                            sync.wait_ge(s_wo[0], 16 * 4 * 3)
                            sync.wait_ge(s_wo[1], 16 * 4 * 3)
                            for mc in range(CH):
                                arena_stream(2, mc)
                        if w == 5:
                            pos, bank, seq = _GINFO[("p", 3, 0, 1, 7)]
                            sync.wait_ge(s_pfb[bank], seq + 1)
                            sync.wait_ge(s_wo[0], 16 * 4 * 4)
                            sync.wait_ge(s_wo[1], 16 * 4 * 4)
                            for mc in range(CH):
                                arena_stream(3, mc)
                    elif u[0] == "B":
                        _, it, b = u
                        if b == 0:
                            for p in range(2):
                                w = p
                                for mc in range(CH):
                                    j = _stream_j0(it, p, mc)
                                    sync.wait_ge(s_wo[0], 16 * 4 * (w + 1))
                                    sync.wait_ge(s_wo[1], 16 * 4 * (w + 1))
                                    if j >= 4:
                                        sync.wait_ge(
                                            s_mm,
                                            _GINFO[_stream_key0(j - 4)][0] + 1)
                                    sync.dma_start(
                                        out=wstr[:, (j % 4) * 1024:
                                                 (j % 4) * 1024 + 1024],
                                        in_=bass.AP(w_hbm,
                                                    w * 1048576 + mc * 131072,
                                                    [[1024, 128], [1, 1024]]),
                                    ).then_inc(s_wl[j % 4], 16)
                    elif u[0] == "P" and u[1:] == (2, 3, 1):
                        if True:
                            for mc in range(CH):
                                sync.wait_ge(s_wo[0], 16 * 4 * 5)
                                sync.wait_ge(s_wo[1], 16 * 4 * 5)
                                sync.wait_ge(s_mm, _GINFO[("p", 3, 1, 0, mc)][0] + 1)
                                arena_stream(4, mc)
                            for mc in range(CH):
                                sync.wait_ge(s_wo[0], 16 * 4 * 6)
                                sync.wait_ge(s_wo[1], 16 * 4 * 6)
                                sync.wait_ge(s_mm, _GINFO[("p", 3, 1, 1, mc)][0] + 1)
                                arena_stream(5, mc)
                sync.wait_ge(s_od[0], 16 * 16)
                sync.wait_ge(s_od[1], 16 * 16)

            @blk.scalar
            def _(scalar):
                def rows_a(inst):
                    scalar.wait_ge(s_r1, inst + 1)
                    scalar.activation(mu2[:], nmu[:], AF.Square).then_inc(s_r2, 1)
                    scalar.wait_ge(s_r3, inst + 1)
                    scalar.activation(mu2[:], nmr[:], AF.Ln, bias=eps_sb[0:1, 0:1])
                    scalar.activation(rstd[:], mu2[:], AF.Exp,
                                      scale=-0.5).then_inc(s_r4, 1)

                def rows_b(inst):
                    scalar.wait_ge(s_bc, inst + 1)
                    scalar.wait_ge(s_apl, 8 * inst)
                    scalar.activation(c1bf[:], ps_c[:], AF.Copy)
                    scalar.activation(c0bf[:], ps_d[:], AF.Copy).then_inc(s_bce, 1)

                def rows_s(inst):
                    rows_a(inst)
                    rows_b(inst)

                def silus(it):
                    for mc in range(CH):
                        pos2, bank2, seq2 = _GINFO[("p", it, 2, 1, mc)]
                        k = it * 8 + mc
                        scalar.wait_ge(s_pfb[bank2], seq2 + 1)
                        if k >= 2:
                            scalar.wait_ge(s_pou, k - 1)
                        scalar.dma_start(
                            out=x2st[:, (k % 2) * TQ:(k % 2) * TQ + TQ],
                            in_=bass.AP(x_fm, mc * 128 * TOK + it * TQ,
                                        [[TOK, 128], [1, TQ]]),
                        ).then_inc(s_lx2[k % 2], 16)
                        if k >= 2:
                            kp = k - 2
                            scalar.dma_start(
                                out=bass.AP(out_fm,
                                            (kp % 8) * 128 * TOK + (kp // 8) * TQ,
                                            [[TOK, 128], [1, TQ]]),
                                in_=outt[:, (kp % 2) * TQ:(kp % 2) * TQ + TQ],
                            ).then_inc(s_od[kp % 2], 16)
                        scalar.activation(gbf[:, (mc % 2) * TQ:(mc % 2) * TQ + TQ],
                                          Sst[:, it * 4096 + mc * TQ:
                                              it * 4096 + mc * TQ + TQ],
                                          AF.Silu).then_inc(s_sil, 1)

                scalar.wait_ge(s_ld, 13 * 16)
                scalar.activation(A_sb[:], va_sb[:], AF.Sigmoid).then_inc(s_wol, 1)
                bufb_open = [False]
                for u in UNITS:
                    if u[0] == "W":
                        continue
                    elif u[0] == "A":
                        tq = u[1]
                        inst = _inst_a(tq)
                        rows_s(inst)
                        if tq == 0:
                            scalar.wait_ge(s_bpr, 1)
                        for mc in range(CH):
                            pos, bank, seq = _GINFO[("h", tq, mc)]
                            scalar.wait_ge(s_mm, pos + 1)
                            scalar.activation(
                                bhbf[:, mc * TOK + tq * TQ: mc * TOK + tq * TQ + TQ],
                                ps_p4[bank][:], AF.Identity,
                                bias=bprod_sb[:, mc:mc + 1],
                                scale=vb_sb[:, mc:mc + 1],
                            ).then_inc(s_pfb[bank], 1)
                    elif u[0] == "B" or u[0] == "P":
                        def zev(it, b, p):
                            zi = _ZINFO[(it, b, p)]
                            scalar.wait_ge(s_zm, zi + 1)
                            if zi >= 1:
                                pk = _ZREV[zi - 1]
                                scalar.wait_ge(
                                    s_mm,
                                    _GINFO[("p", pk[0], pk[1], pk[2], 7)][0] + 1)
                            scalar.activation(zbf[:], ps_z[:],
                                              AF.Copy).then_inc(s_ze, 1)

                        def evacs_rows(it, b, inst, defer_bc=False):
                            if _SECOND.get(inst) and not bufb_open[0]:
                                scalar.wait_ge(s_p2, 96)
                                bufb_open[0] = True
                            for mc in range(CH):
                                pos, bank, seq = _GINFO[("p", it, b, 0, mc)]
                                scalar.wait_ge(s_mm, pos + 1)
                                scalar.activation(
                                    y1t(inst, mc),
                                    ps_p4[bank][:], AF.Identity,
                                    bias=bias_sb[:, 2 * b * 8 + mc:
                                                 2 * b * 8 + mc + 1],
                                ).then_inc(s_pfb[bank], 1)
                            rows_a(inst)
                            if not defer_bc:
                                rows_b(inst)

                        def gelus(it, b, inst):
                            for mc in range(CH):
                                k = inst * 8 + mc
                                scalar.wait_ge(s_apl, k + 1)
                                scalar.wait_ge(s_stp, k + 1)
                                scalar.activation(
                                    y1t(inst, mc),
                                    t2bf[:, (mc % 4) * TQ:(mc % 4) * TQ + TQ],
                                    AF.Gelu,
                                    scale=lng_sb[:, b * 8 + mc:b * 8 + mc + 1],
                                    bias=lnb_sb[:, b * 8 + mc:b * 8 + mc + 1],
                                ).then_inc(s_gel, 1)

                        if u[0] == "B":
                            _, it, b = u
                            inst = _inst_b(it, b)
                            zev(it, b, 0)
                            evacs_rows(it, b, inst)
                            gelus(it, b, inst)
                            zev(it, b, 1)
                            if b == 2:
                                silus(it)
                        else:
                            _, itX, itY, b = u
                            iX, iY = _inst_b(itX, b), _inst_b(itY, b)
                            zev(itX, b, 0)
                            evacs_rows(itX, b, iX)
                            zev(itY, b, 0)
                            gelus(itX, b, iX)
                            evacs_rows(itY, b, iY, defer_bc=True)
                            zev(itX, b, 1)
                            rows_b(iY)
                            gelus(itY, b, iY)
                            zev(itY, b, 1)
                            if b == 2:
                                silus(itX)
                                silus(itY)
                                if itY == 3:
                                    for kp in (30, 31):
                                        scalar.wait_ge(s_pou, kp + 1)
                                        scalar.dma_start(
                                            out=bass.AP(out_fm,
                                                        (kp % 8) * 128 * TOK
                                                        + (kp // 8) * TQ,
                                                        [[TOK, 128], [1, TQ]]),
                                            in_=outt[:, (kp % 2) * TQ:
                                                     (kp % 2) * TQ + TQ],
                                        ).then_inc(s_od[kp % 2], 16)

            @blk.gpsimd
            def _(gpsimd):
                gp_units = ([("A", 0), ("A", 1)]
                            + [u for u in UNITS if u not in (("A", 0), ("A", 1))])
                qsem = [s_q, s_qc]

                def qload(w, kc):
                    g = w * 8 + kc
                    gpsimd.dma_start(
                        out=qbf[:, (kc % 2) * 1024:(kc % 2) * 1024 + 1024],
                        in_=bass.AP(q_t, w * D * D + kc * 128 * D,
                                    [[D, 128], [1, D]]),
                    ).then_inc(qsem[g % 2], 16)

                def esload(w, kc):
                    g = w * 8 + kc
                    gpsimd.dma_start(
                        out=esbf[:, (g % 2) * 1024:(g % 2) * 1024 + 1024],
                        in_=bass.AP(sc_t, w * 64 * D + kc * 8 * D,
                                    [[D, 8], [0, 16], [1, D]]),
                    ).then_inc(s_es2[g % 2], 16)

                for u in gp_units:
                    if u[0] == "W":
                        w = u[1]
                        if w == 0:
                            gpsimd.dma_start(
                                out=wio[:],
                                in_=bass.AP(wint_h, 0,
                                            [[D, 128], [128 * D, CH], [1, D]]),
                            ).then_inc(s_woc, 16)
                        qload(w, 0)
                        qload(w, 1)
                        esload(w, 0)
                        for kc in range(CH):
                            i = w * 8 + kc
                            hfq = (i % 2) * 1024
                            gpsimd.wait_ge(qsem[i % 2], 16 * (i // 2 + 1))
                            gpsimd.tensor_scalar(
                                qcbf[:, hfq:hfq + 1024],
                                qbf[:, hfq:hfq + 1024],
                                1.0 / 7.5, -1.0, OP.mult, OP.add)
                            if kc + 2 < CH:
                                qload(w, kc + 2)
                            if kc + 1 < CH:
                                esload(w, kc + 1)
                            gpsimd.wait_ge(s_es2[i % 2], 16 * (i // 2 + 1))
                            if i >= 2:
                                gpsimd.wait_ge(s_wo[i % 2], 16 * (i // 2))
                            gpsimd.tensor_mul(
                                wchk[:, hfq:hfq + 1024],
                                qcbf[:, hfq:hfq + 1024],
                                esbf[:, hfq:hfq + 1024],
                            ).then_inc(s_p2, 2)
                    elif u[0] == "A":
                        tq = u[1]
                        xdst = xbf if tq % 2 == 0 else ybf
                        for p in range(4):
                            kp = tq * 4 + p
                            if tq >= 2:
                                gpsimd.wait_ge(s_apl,
                                               8 * _inst_a(tq - 2) + 2 * (p + 1))
                            if kp >= 2:
                                gpsimd.wait_ge(s_lx[kp % 2], 16 * (kp // 2))
                            gpsimd.dma_start(
                                out=xdst[:, 2 * p * TQ:2 * p * TQ + 2 * TQ],
                                in_=bass.AP(x_fm, 2 * p * 128 * TOK + tq * TQ,
                                            [[TOK, 128], [128 * TOK, 2], [1, TQ]]),
                            ).then_inc(s_lx[kp % 2], 16)
                        if tq == 1:
                            for w in range(NW):
                                gpsimd.dma_start(
                                    out=latall[:, w * 256:(w + 1) * 256],
                                    in_=bass.AP(lat_h, w * D * 32,
                                                [[32, 128], [128 * 32, 8],
                                                 [1, 32]]),
                                ).then_inc(s_lg, 16)
                            gpsimd.dma_start(out=lbt_sb[:],
                                             in_=lbt_h.ap()).then_inc(s_lg, 16)
                        if tq == 3:
                            gpsimd.wait_ge(s_mm, _GINFO[("h", 3, 7)][0] + 1)
                            gpsimd.dma_start(
                                out=wio[:],
                                in_=bass.AP(woutt_h, 0,
                                            [[D, 128], [128 * D, CH], [1, D]]),
                            ).then_inc(s_woc, 16)
                            gpsimd.wait_ge(s_scn2, 32)
                            gpsimd.dma_start(out=f_bnc.ap(),
                                             in_=stA[:, 24:32]).then_inc(s_cc, 16)
                            gpsimd.wait_ge(s_cc, 16)
                            gpsimd.collective_compute(
                                "AllGather", OP.bypass,
                                replica_groups=[list(range(NCORE))],
                                ins=[f_bnc.ap().opt()],
                                outs=[g_bnc.ap().opt()],
                            ).then_inc(s_cc2, 1)
                            gpsimd.wait_ge(s_cc2, 1)
                            gpsimd.dma_start(
                                out=gg_sb[:],
                                in_=bass.AP(g_bnc, 0, [[8, 128], [1024, 8], [1, 8]]),
                            ).then_inc(s_g, 16)

            @blk.tensor
            def _(tensor):
                tensor.wait_ge(s_ld, 13 * 16)

                def proj_group(key, lhsT_tile, rhs_tiles, lora_w=None):
                    pos, bank, seq = _GINFO[key]
                    if seq >= 1:
                        tensor.wait_ge(s_pfb[bank], seq)
                    last = 7 if lora_w is None else 8
                    for kc in range(CH):
                        mm = tensor.matmul(
                            ps_p4[bank][:], lhsT_tile(kc), rhs_tiles(kc),
                            start=(kc == 0), stop=(kc == last))
                    if lora_w is not None:
                        w, mc = lora_w
                        mm = tensor.matmul(ps_p4[bank][:], lbT_slice(w, mc), zbf[:],
                                           start=False, stop=True)
                    mm.then_inc(s_mm, 1)

                def zmms(it, b, p, rhs_tiles):
                    zi = _ZINFO[(it, b, p)]
                    w = 2 * b + p
                    if zi >= 1:
                        tensor.wait_ge(s_ze, zi)
                    for kc in range(CH):
                        mm = tensor.matmul(
                            ps_z[:],
                            latall[:, w * 256 + kc * 32: w * 256 + kc * 32 + 32],
                            rhs_tiles(kc),
                            start=(kc == 0), stop=(kc == 7))
                    mm.then_inc(s_zm, 1)

                def stats_pair(inst, c, src_bf):
                    k = inst * 8 + c
                    tensor.wait_ge(s_sq2, k + 1)
                    tensor.matmul(ps_st[:], no_sb[:], src_bf,
                                  start=(c == 0), stop=False)
                    mm = tensor.matmul(ps_st[:], po_sb[:],
                                       sqs[:, (c % 2) * TQ:(c % 2) * TQ + TQ],
                                       start=False, stop=(c == 7))
                    mm.then_inc(s_stp, 1)

                def bcast_inst(inst):
                    tensor.wait_ge(s_r4, inst + 1)
                    tensor.wait_ge(s_r5, inst + 1)
                    tensor.wait_ge(s_bce, inst)
                    tensor.matmul(ps_c[:], bco_sb[:], rstd[:], start=True, stop=True)
                    tensor.matmul(ps_d[:], bco_sb[:], nmr[:],
                                  start=True, stop=True).then_inc(s_bc, 1)

                def sproj(it):
                    tensor.wait_ge(s_woc, 32)
                    tensor.wait_ge(s_csc, 8 * (it + 1))
                    for mc in range(CH):
                        proj_group(
                            ("s", it, mc),
                            lambda kc, mc=mc: wio[:, kc * D + mc * 128:
                                                  kc * D + mc * 128 + 128],
                            lambda kc: ybf[:, kc * TQ:(kc + 1) * TQ])

                def wtile(it, b, p, mc):
                    w = 2 * b + p
                    if b == 0:
                        j = _stream_j0(it, p, mc)
                        tensor.wait_ge(s_wl[j % 4], 16 * (j // 4 + 1))
                        return lambda kc, j=j: wstr[
                            :, (j % 4) * 1024 + kc * 128:
                            (j % 4) * 1024 + kc * 128 + 128]
                    tensor.wait_ge(s_ar[mc % 2], 16 * (_ar_cnt(w, mc) + 1))
                    base = mc * 2048 + (w % 2) * 1024
                    return lambda kc, base=base: xnbf[
                        :, base + kc * 128: base + kc * 128 + 128]

                def t_first(it, b, inst, do_bcast=True):
                    if b == 0:
                        rhs = lambda kc: xnbf[:, kc * TOK + it * TQ:
                                              kc * TOK + it * TQ + TQ]
                    else:
                        need = {}
                        for mc in range(CH):
                            pos, bank, seq = _GINFO[("p", it, b - 1, 1, mc)]
                            need[bank] = max(need.get(bank, -1), seq)
                        for bank in sorted(need):
                            tensor.wait_ge(s_pfb[bank], need[bank] + 1)
                        rhs = lambda kc: Sst[:, it * 4096 + kc * TQ:
                                             it * 4096 + kc * TQ + TQ]
                    tensor.wait_ge(s_r1, inst)
                    if (it, b) == (0, 0):
                        tensor.wait_ge(s_lg, 112)
                    zmms(it, b, 0, rhs)
                    zi1 = _ZINFO[(it, b, 0)]
                    for mc in range(CH):
                        lt = wtile(it, b, 0, mc)
                        if mc == 0:
                            tensor.wait_ge(s_ze, zi1 + 1)
                        proj_group(("p", it, b, 0, mc), lt, rhs,
                                   lora_w=(2 * b, mc))
                        if mc >= 2:
                            stats_pair(inst, mc - 2, y1t(inst, mc - 2))
                    for c in (6, 7):
                        stats_pair(inst, c, y1t(inst, c))
                    if do_bcast:
                        bcast_inst(inst)

                def t_second(it, b, inst):
                    tensor.wait_ge(s_gel, 8 * (_GPRE[inst] + 1))
                    zmms(it, b, 1, lambda kc: y1t(inst, kc))
                    zi2 = _ZINFO[(it, b, 1)]
                    for mc in range(CH):
                        lt = wtile(it, b, 1, mc)
                        if mc == 0:
                            tensor.wait_ge(s_ze, zi2 + 1)
                        proj_group(("p", it, b, 1, mc), lt,
                                   lambda kc: y1t(inst, kc),
                                   lora_w=(2 * b + 1, mc))

                for u in UNITS:
                    if u[0] == "W":
                        continue
                    if u[0] == "A":
                        tq = u[1]
                        xsrc = xbf if tq % 2 == 0 else ybf
                        inst = _inst_a(tq)
                        tensor.wait_ge(s_r1, inst)
                        for c in range(CH):
                            stats_pair(inst, c, xsrc[:, c * TQ:(c + 1) * TQ])
                        bcast_inst(inst)
                        tensor.wait_ge(s_apl, 8 * (inst + 1))
                        if tq == 0:
                            tensor.wait_ge(s_woc, 16)
                        for mc in range(CH):
                            proj_group(
                                ("h", tq, mc),
                                lambda kc, mc=mc: wio[:, kc * D + mc * 128:
                                                      kc * D + mc * 128 + 128],
                                lambda kc: xnbf[:, kc * TOK + tq * TQ:
                                                kc * TOK + tq * TQ + TQ])
                    elif u[0] == "B":
                        _, it, b = u
                        inst = _inst_b(it, b)
                        t_first(it, b, inst)
                        t_second(it, b, inst)
                    else:
                        _, itX, itY, b = u
                        iX, iY = _inst_b(itX, b), _inst_b(itY, b)
                        t_first(itX, b, iX)
                        t_first(itY, b, iY)
                        t_second(itX, b, iX)
                        t_second(itY, b, iY)
                        if b == 2:
                            sproj(itX)
                            sproj(itY)

            @blk.vector
            def _(vector):
                vc = [0]

                def g(instr):
                    instr.then_inc(s_vch, 1)
                    vc[0] += 1
                    vector.wait_ge(s_vch, vc[0])

                g(vector.memset(eps_sb[:], EPS))
                g(vector.memset(neg1_sb[:], -1.0))
                vector.wait_ge(s_wol, 1)
                bufs = [ptmp, ptmp2]
                g(vector.tensor_mul(ptmp[:], A_sb[:], A_sb[:]))
                cur = ptmp
                for i in range(8):
                    dst = a512 if i == 7 else bufs[(i + 1) % 2]
                    g(vector.tensor_mul(dst[:], cur[:], cur[:]))
                    cur = dst
                g(vector.tensor_mul(a1024[:], a512[:], a512[:]))
                g(vector.tensor_mul(a1536[:], a1024[:], a512[:]))
                g(vector.tensor_mul(p2048[:], a1024[:], a1024[:]))
                g(vector.tensor_mul(p4096[:], p2048[:], p2048[:]))
                vector.tensor_mul(bprod_sb[:], vb_sb[:],
                                  vbin_sb[:]).then_inc(s_bpr, 1)
                g(vector.tensor_copy(
                    crep[:], bass.AP(p2048.ap().tensor, 0, [[8, 128], [0, 8], [1, 8]])))
                g(vector.tensor_mul(u_sb[:], mp_sb[:], crep[:]))
                g(vector.tensor_add(gg_sb[:], m1_sb[:], u_sb[:]))
                g(vector.tensor_copy(
                    crep[:], bass.AP(p4096.ap().tensor, 0, [[8, 128], [0, 8], [1, 8]])))
                g(vector.tensor_mul(u_sb[:], mp2_sb[:], crep[:]))
                g(vector.tensor_add(coef[:], gg_sb[:], u_sb[:]))

                def rows_inst(inst):
                    vector.wait_ge(s_stp, 8 * (inst + 1))
                    vector.tensor_copy(nmu[:], ps_st[0:1, :])
                    vector.tensor_copy(msq[:], ps_st[32:33, :]).then_inc(s_r1, 1)
                    vector.wait_ge(s_r2, inst + 1)
                    vector.tensor_sub(nmr[:], msq[:], mu2[:]).then_inc(s_r3, 1)
                    vector.wait_ge(s_r4, inst + 1)
                    vector.tensor_mul(nmr[:], nmu[:], rstd[:]).then_inc(s_r5, 1)

                def apply_inst(inst, src_bf, dst, gel_guard=False):
                    vector.wait_ge(s_bce, inst + 1)
                    for c in range(CH):
                        if gel_guard:
                            g = 8 * _GPRE[inst] + c - 3
                            if g > 0:
                                vector.wait_ge(s_gel, g)
                        vector.tensor_mul(t1bf[:], src_bf(c), c1bf[:])
                        vector.tensor_add(dst(c), t1bf[:], c0bf[:]).then_inc(s_apl, 1)

                def wait_evac(key):
                    pos, bank, seq = _GINFO[key]
                    vector.wait_ge(s_pfb[bank], seq + 1)

                def pstt_out(it):
                    for mc in range(CH):
                        pos, bank, seq = _GINFO[("s", it, mc)]
                        k = it * 8 + mc
                        vector.wait_ge(s_mm, pos + 1)
                        vector.wait_ge(s_sil, k + 1)
                        vector.scalar_tensor_tensor(
                            tpbf[:], ps_p4[bank][:],
                            vbout_sb[:, mc:mc + 1],
                            gbf[:, (mc % 2) * TQ:(mc % 2) * TQ + TQ],
                            OP.add, OP.mult,
                        ).then_inc(s_pfb[bank], 1)
                        vector.wait_ge(s_lx2[k % 2], 16 * (k // 2 + 1))
                        if k >= 2:
                            vector.wait_ge(s_od[k % 2], 16 * (k // 2))
                        vector.tensor_add(outt[:, (k % 2) * TQ:(k % 2) * TQ + TQ],
                                          x2st[:, (k % 2) * TQ:(k % 2) * TQ + TQ],
                                          tpbf[:]).then_inc(s_pou, 1)

                def cscans(it):
                    if it > 0:
                        vector.wait_ge(s_mm, _GINFO[("s", it - 1, 7)][0] + 1)
                    for c in range(CH):
                        init = (carry[:, c:c + 1] if it == 0
                                else stC[:, it * 8 + c: it * 8 + c + 1])
                        vector.tensor_tensor_scan(
                            ybf[:, c * TQ:(c + 1) * TQ], a_bcast(c, TQ),
                            bhbf[:, c * TOK + it * TQ: c * TOK + it * TQ + TQ],
                            init, OP.mult, OP.add).then_inc(s_csc, 1)

                for u in UNITS:
                    if u[0] == "W":
                        continue
                    elif u[0] == "A":
                        tq = u[1]
                        xsrc = xbf if tq % 2 == 0 else ybf
                        inst = _inst_a(tq)
                        for c in range(CH):
                            k = inst * 8 + c
                            kp = tq * 4 + c // 2
                            vector.wait_ge(s_lx[kp % 2], 16 * (kp // 2 + 1))
                            if k >= 2:
                                vector.wait_ge(s_stp, k - 1)
                            vector.tensor_mul(sqs[:, (c % 2) * TQ:(c % 2) * TQ + TQ],
                                              xsrc[:, c * TQ:(c + 1) * TQ],
                                              xsrc[:, c * TQ:(c + 1) * TQ]).then_inc(s_sq2, 1)
                        rows_inst(inst)
                        apply_inst(
                            inst,
                            lambda c: xsrc[:, c * TQ:(c + 1) * TQ],
                            lambda c: xnbf[:, c * TOK + tq * TQ: c * TOK + tq * TQ + TQ])
                        for c in range(CH):
                            wait_evac(("h", tq, c))
                            init = 0.0 if tq == 0 else stA[:, (tq - 1) * 8 + c:
                                                           (tq - 1) * 8 + c + 1]
                            vector.tensor_tensor_scan(
                                tpbf[:], a_bcast(c, TQ),
                                bhbf[:, c * TOK + tq * TQ: c * TOK + tq * TQ + TQ],
                                init, OP.mult, OP.add).then_inc(s_scn, 1)
                            vector.wait_ge(s_scn, tq * 8 + c + 1)
                            vector.tensor_copy(stA[:, tq * 8 + c: tq * 8 + c + 1],
                                               tpbf[:, TQ - 1:TQ]).then_inc(s_scn2, 1)
                    elif u[0] == "B" or u[0] == "P":
                        def v_first(it, b, inst):
                            for mc in range(CH):
                                k = inst * 8 + mc
                                wait_evac(("p", it, b, 0, mc))
                                if k >= 2:
                                    vector.wait_ge(s_stp, k - 1)
                                vector.tensor_mul(
                                    sqs[:, (mc % 2) * TQ:(mc % 2) * TQ + TQ],
                                    y1t(inst, mc),
                                    y1t(inst, mc)).then_inc(s_sq2, 1)
                            rows_inst(inst)
                            apply_inst(inst,
                                       lambda c: y1t(inst, c),
                                       lambda c: t2bf[:, (c % 4) * TQ:
                                                      (c % 4) * TQ + TQ],
                                       gel_guard=True)
                        def v_second(it, b, inst):
                            for mc in range(CH):
                                pos, bank, seq = _GINFO[("p", it, b, 1, mc)]
                                vector.wait_ge(s_mm, pos + 1)
                                sdst = Sst[:, it * 4096 + mc * TQ:
                                           it * 4096 + mc * TQ + TQ]
                                if b == 0:
                                    res_in = xnbf[:, mc * TOK + it * TQ:
                                                  mc * TOK + it * TQ + TQ]
                                else:
                                    res_in = sdst
                                vector.scalar_tensor_tensor(
                                    sdst, ps_p4[bank][:],
                                    bias_sb[:, (2 * b + 1) * 8 + mc:
                                            (2 * b + 1) * 8 + mc + 1],
                                    res_in, OP.add, OP.add,
                                ).then_inc(s_pfb[bank], 1)

                        if u[0] == "B":
                            _, it, b = u
                            inst = _inst_b(it, b)
                            v_first(it, b, inst)
                            v_second(it, b, inst)
                        else:
                            _, itX, itY, b = u
                            iX, iY = _inst_b(itX, b), _inst_b(itY, b)
                            v_first(itX, b, iX)
                            v_first(itY, b, iY)
                            v_second(itX, b, iX)
                            v_second(itY, b, iY)
                            if b == 2:
                                pstt_out(itX)
                                if itX + 1 < 4:
                                    cscans(itX + 1)
                                pstt_out(itY)
                                if itY + 1 < 4:
                                    cscans(itY + 1)
                        if u[0] == "P" and u[1:] == (0, 1, 1):
                            vector.wait_ge(s_g, 16)
                            vcc = [0]

                            def gc(instr):
                                instr.then_inc(s_vch2, 1)
                                vcc[0] += 1
                                vector.wait_ge(s_vch2, vcc[0])

                            gc(vector.tensor_mul(u_sb[:], gg_sb[:], coef[:]))
                            cbufs = [carry, carry2]
                            gc(vector.tensor_add(carry[:], u_sb[:, 0:8],
                                                 u_sb[:, 8:16]))
                            cur = 0
                            for j in range(2, 8):
                                gc(vector.tensor_add(cbufs[(cur + 1) % 2][:],
                                                     cbufs[cur][:],
                                                     u_sb[:, j * 8:(j + 1) * 8]))
                                cur = (cur + 1) % 2
                            if cur != 0:
                                gc(vector.tensor_copy(carry[:], carry2[:]))
                            for tq2, ap in ((1, a512), (2, a1024), (3, a1536)):
                                gc(vector.tensor_mul(ptmp[:], ap[:], carry[:]))
                                gc(vector.tensor_add(stC[:, tq2 * 8:(tq2 + 1) * 8],
                                                     stA[:, (tq2 - 1) * 8:tq2 * 8],
                                                     ptmp[:]))
                            cscans(0)

    return nc


_NC_CACHE = {}


def _get_nc():
    if "nc" not in _NC_CACHE:
        _NC_CACHE["nc"] = build_nc()
    return _NC_CACHE["nc"]


def _vec128(v):
    return np.ascontiguousarray(np.asarray(v, dtype=np.float32).reshape(8, 128).T)


def kernel(L0_attn_x, L0_attn_w_in, L0_attn_b_in, L0_attn_a, L0_attn_b,
           L0_attn_w_out, L0_attn_b_out, L0_ffn_qweights, L0_ffn_scales,
           L0_ffn_biases, L0_ffn_lora_a, L0_ffn_lora_b, L0_ffn_ln_gamma,
           L0_ffn_ln_beta):
    x = np.asarray(L0_attn_x, dtype=np.float32).reshape(16384, D)
    q_t = np.ascontiguousarray(np.asarray(L0_ffn_qweights, dtype=np.int32)
                               .transpose(0, 2, 1))
    sc_t = np.ascontiguousarray(np.asarray(L0_ffn_scales, dtype=np.float32)
                                .transpose(0, 2, 1))
    la = np.asarray(L0_ffn_lora_a, dtype=np.float32)
    lat = np.ascontiguousarray(la.transpose(0, 2, 1))
    lbt = np.asarray(L0_ffn_lora_b, dtype=np.float32).transpose(0, 2, 1)
    lbt_all = np.ascontiguousarray(lbt.transpose(1, 0, 2).reshape(32, NW * D))
    biases = np.ascontiguousarray(
        np.asarray(L0_ffn_biases, dtype=np.float32).reshape(NW, 8, 128)
        .transpose(0, 2, 1))
    lng = np.ascontiguousarray(
        np.asarray(L0_ffn_ln_gamma, dtype=np.float32).reshape(3, 8, 128)
        .transpose(0, 2, 1))
    lnb = np.ascontiguousarray(
        np.asarray(L0_ffn_ln_beta, dtype=np.float32).reshape(3, 8, 128)
        .transpose(0, 2, 1))
    va = _vec128(L0_attn_a)
    vb = _vec128(L0_attn_b)
    vbin = _vec128(L0_attn_b_in)
    vbout = _vec128(L0_attn_b_out)
    wint = np.ascontiguousarray(np.asarray(L0_attn_w_in, dtype=np.float32).T)
    woutt = np.ascontiguousarray(np.asarray(L0_attn_w_out, dtype=np.float32).T)

    in_maps = []
    for c in range(NCORE):
        k = c % 4
        m1 = np.zeros((128, 64), np.float32)
        mp = np.zeros((128, 64), np.float32)
        mp2 = np.zeros((128, 64), np.float32)
        base = (c // 4) * 4
        if k >= 1:
            m1[:, (base + k - 1) * 8:(base + k) * 8] = 1.0
        if k >= 2:
            mp[:, (base + k - 2) * 8:(base + k - 1) * 8] = 1.0
        if k >= 3:
            mp2[:, (base + k - 3) * 8:(base + k - 2) * 8] = 1.0
        x_fmc = np.ascontiguousarray(x[c * TOK:(c + 1) * TOK, :].T)
        in_maps.append(dict(
            x_fm=x_fmc, q_t=q_t, sc_t=sc_t, lat_h=lat, lbt_h=lbt_all,
            bias_h=biases, lng_h=lng, lnb_h=lnb,
            va_h=va, vb_h=vb, vbin_h=vbin, vbout_h=vbout,
            wint_h=wint, woutt_h=woutt,
            m1_h=m1, mp_h=mp, mp2_h=mp2,
        ))

    nc = _get_nc()
    res = run_bass_kernel_spmd(nc, in_maps, core_ids=list(range(NCORE)))
    out = np.empty((16384, D), dtype=np.float32)
    for c in range(NCORE):
        out[c * TOK:(c + 1) * TOK, :] = res.results[c]["out_fm"].T
    return out.reshape(2, 8192, D)


# version 22
# speedup vs baseline: 1.0855x; 1.0855x over previous
import contextlib
import numpy as np
import ml_dtypes
import concourse.bass as bass
import concourse.mybir as mybir
from concourse.bass_utils import run_bass_kernel_spmd

F32 = mybir.dt.float32
BF16 = mybir.dt.bfloat16
I32 = mybir.dt.int32
AF = mybir.ActivationFunctionType
OP = mybir.AluOpType

NCORE = 8
TOK = 2048
D = 1024
CH = 8
TQ = 512
NTQ = 4
NW = 6
EPS = 1e-5
BF = ml_dtypes.bfloat16

UNITS = [("W", 0), ("W", 1), ("A", 0), ("A", 1), ("B", 0, 0), ("W", 2),
         ("A", 2), ("B", 1, 0), ("W", 3), ("A", 3), ("B", 2, 0), ("W", 4),
         ("B", 3, 0), ("W", 5),
         ("P", 0, 1, 1), ("P", 2, 3, 1), ("P", 0, 1, 2), ("P", 2, 3, 2)]


def _build_tables():
    ginfo = {}
    inst_of = {}
    gelu_pre = {}
    zinfo = {}
    second = {}
    order = []
    ninst = 0
    nb = 0
    nz = 0
    for u in UNITS:
        if u[0] == "W":
            continue
        if u[0] == "A":
            tq = u[1]
            inst_of[u] = ninst
            gelu_pre[ninst] = nb
            ninst += 1
            for mc in range(8):
                order.append(("h", tq, mc))
        elif u[0] == "B":
            _, it, b = u
            inst_of[u] = ninst
            gelu_pre[ninst] = nb
            ninst += 1
            nb += 1
            zinfo[(it, b, 0)] = nz
            nz += 1
            for mc in range(8):
                order.append(("p", it, b, 0, mc))
            zinfo[(it, b, 1)] = nz
            nz += 1
            for mc in range(8):
                order.append(("p", it, b, 1, mc))
        else:
            _, itX, itY, b = u
            for idx, it in enumerate((itX, itY)):
                inst_of[("B", it, b)] = ninst
                gelu_pre[ninst] = nb
                if idx == 1:
                    second[ninst] = True
                ninst += 1
                nb += 1
            for it in (itX, itY):
                zinfo[(it, b, 0)] = nz
                nz += 1
            for it in (itX, itY):
                for mc in range(8):
                    order.append(("p", it, b, 0, mc))
            for it in (itX, itY):
                zinfo[(it, b, 1)] = nz
                nz += 1
            for it in (itX, itY):
                for mc in range(8):
                    order.append(("p", it, b, 1, mc))
            if b == 2:
                for it in (itX, itY):
                    for mc in range(8):
                        order.append(("s", it, mc))
    ginfo_banks = [0, 0, 0, 0]
    for pos, key in enumerate(order):
        bank = pos % 4
        ginfo[key] = (pos, bank, ginfo_banks[bank])
        ginfo_banks[bank] += 1
    return ginfo, inst_of, gelu_pre, zinfo, second


_GINFO, _INST, _GPRE, _ZINFO, _SECOND = _build_tables()
_ZREV = {v: k for k, v in _ZINFO.items()}


def _inst_a(tq):
    return _INST[("A", tq)]


def _inst_b(it, b):
    return _INST[("B", it, b)]


def _stream_j0(it, p, mc):
    return it * 16 + p * 8 + mc


def _stream_key0(j):
    return ("p", j // 16, 0, (j % 16) // 8, j % 8)


def _ar_cnt(w, mc):
    return (w - 2) * 4 + mc // 2


def build_nc():
    nc = bass.Bass("TRN2", target_bir_lowering=False, detect_race_conditions=False)

    x_fm = nc.dram_tensor("x_fm", [D, TOK], F32, kind="ExternalInput")
    q_t = nc.dram_tensor("q_t", [NW, D, D], I32, kind="ExternalInput")
    sc_t = nc.dram_tensor("sc_t", [NW, 64, D], F32, kind="ExternalInput")
    lat_h = nc.dram_tensor("lat_h", [NW, D, 32], F32, kind="ExternalInput")
    lbt_h = nc.dram_tensor("lbt_h", [32, NW * D], F32, kind="ExternalInput")
    bias_h = nc.dram_tensor("bias_h", [NW, 128, 8], F32, kind="ExternalInput")
    lng_h = nc.dram_tensor("lng_h", [3, 128, 8], F32, kind="ExternalInput")
    lnb_h = nc.dram_tensor("lnb_h", [3, 128, 8], F32, kind="ExternalInput")
    va_h = nc.dram_tensor("va_h", [128, 8], F32, kind="ExternalInput")
    vb_h = nc.dram_tensor("vb_h", [128, 8], F32, kind="ExternalInput")
    vbin_h = nc.dram_tensor("vbin_h", [128, 8], F32, kind="ExternalInput")
    vbout_h = nc.dram_tensor("vbout_h", [128, 8], F32, kind="ExternalInput")
    wint_h = nc.dram_tensor("wint_h", [D, D], F32, kind="ExternalInput")
    woutt_h = nc.dram_tensor("woutt_h", [D, D], F32, kind="ExternalInput")
    m1_h = nc.dram_tensor("m1_h", [128, 64], F32, kind="ExternalInput")
    mp_h = nc.dram_tensor("mp_h", [128, 64], F32, kind="ExternalInput")
    mp2_h = nc.dram_tensor("mp2_h", [128, 64], F32, kind="ExternalInput")
    out_fm = nc.dram_tensor("out_fm", [D, TOK], F32, kind="ExternalOutput")

    w_hbm = nc.dram_tensor("w_hbm", [NW * 8 * 128 * 8 * 128], BF16)
    f_bnc = nc.dram_tensor("f_bnc", [128, 8], F32)
    g_bnc = nc.dram_tensor("g_bnc", [128 * NCORE, 8], F32)

    no33 = np.zeros((128, 33), dtype=BF)
    no33[:, 0] = -1.0 / D
    po33 = np.zeros((128, 33), dtype=BF)
    po33[:, 32] = 1.0 / D
    no33_c = nc.inline_tensor(no33, "no33c")
    po33_c = nc.inline_tensor(po33, "po33c")
    bcones = nc.inline_tensor(np.ones((1, 128), dtype=BF), "bconesc")

    es = contextlib.ExitStack()
    with es:
        sb = lambda name, shape, dt: es.enter_context(nc.sbuf_tensor(name, shape, dt))
        psb = lambda name, shape: es.enter_context(nc.psum_tensor(name, shape, F32))
        sem = lambda name: es.enter_context(nc.semaphore(name))

        xnbf = sb("xnbf", [128, CH * TOK], BF16)
        bhbf = sb("bhbf", [128, CH * TOK], BF16)
        Sst = sb("Sst", [128, NTQ * CH * TQ], BF16)
        wio = sb("wio", [128, CH * D], BF16)
        wstr = sb("wstr", [128, 4 * 1024], BF16)
        qbf = sb("qbf", [128, 2 * 1024], BF16)
        qcbf = sb("qcbf", [128, 2 * 1024], BF16)
        esbf = sb("esbf", [128, 2 * 1024], BF16)
        wchk = sb("wchk", [128, 2 * 1024], BF16)
        latall = sb("latall", [128, NW * 256], BF16)
        lbt_sb = sb("lbt_sb", [32, NW * D], BF16)
        zbf = sb("zbf", [32, TQ], BF16)
        bias_sb = sb("bias_sb", [128, NW * 8], F32)
        lng_sb = sb("lng_sb", [128, 24], F32)
        lnb_sb = sb("lnb_sb", [128, 24], F32)
        va_sb = sb("va_sb", [128, 8], F32)
        vb_sb = sb("vb_sb", [128, 8], F32)
        vbin_sb = sb("vbin_sb", [128, 8], F32)
        vbout_sb = sb("vbout_sb", [128, 8], F32)
        bprod_sb = sb("bprod_sb", [128, 8], F32)
        A_sb = sb("A_sb", [128, 8], F32)
        a512 = sb("a512", [128, 8], F32)
        a1024 = sb("a1024", [128, 8], F32)
        a1536 = sb("a1536", [128, 8], F32)
        p2048 = sb("p2048", [128, 8], F32)
        p4096 = sb("p4096", [128, 8], F32)
        ptmp = sb("ptmp", [128, 8], F32)
        ptmp2 = sb("ptmp2", [128, 8], F32)
        carry = sb("carry", [128, 8], F32)
        carry2 = sb("carry2", [128, 8], F32)
        coef = sb("coef", [128, 64], F32)
        crep = sb("crep", [128, 64], F32)
        m1_sb = sb("m1_sb", [128, 64], F32)
        mp_sb = sb("mp_sb", [128, 64], F32)
        mp2_sb = sb("mp2_sb", [128, 64], F32)
        gg_sb = sb("gg_sb", [128, 64], F32)
        u_sb = sb("u_sb", [128, 64], F32)
        stA = sb("stA", [128, 32], F32)
        stC = sb("stC", [128, 32], F32)
        no_sb = sb("no_sb", [128, 33], BF16)
        po_sb = sb("po_sb", [128, 33], BF16)
        bco_sb = sb("bco_sb", [1, 128], BF16)
        eps_sb = sb("eps_sb", [1, 1], F32)
        neg1_sb = sb("neg1_sb", [128, 1], F32)
        nmu = sb("nmu", [1, TQ], F32)
        msq = sb("msq", [1, TQ], F32)
        mu2 = sb("mu2", [1, TQ], F32)
        rstd = sb("rstd", [1, TQ], BF16)
        nmr = sb("nmr", [1, TQ], BF16)
        c1bf = sb("c1bf", [128, TQ], BF16)
        c0bf = sb("c0bf", [128, TQ], BF16)
        xbf = sb("xbf", [128, CH * TQ], BF16)
        sqs = sb("sqs", [128, 2 * TQ], BF16)
        y1bf = sb("y1bf", [128, CH * TQ], BF16)
        t1bf = sb("t1bf", [128, TQ], BF16)
        t2bf = sb("t2bf", [128, 4 * TQ], BF16)
        ybf = sb("ybf", [128, CH * TQ], BF16)
        gbf = sb("gbf", [128, 2 * TQ], BF16)
        tpbf = sb("tpbf", [128, TQ], BF16)
        x2st = sb("x2st", [128, 2 * TQ], F32)
        outt = sb("outt", [128, 2 * TQ], F32)

        ps_a = psb("ps_a", [128, 512])
        ps_b = psb("ps_b", [128, 512])
        ps_e = psb("ps_e", [128, 512])
        ps_f = psb("ps_f", [128, 512])
        ps_c = psb("ps_c", [128, 512])
        ps_d = psb("ps_d", [128, 512])
        ps_st = psb("ps_st", [33, 512])
        ps_z = psb("ps_z", [32, 512])
        ps_p4 = [ps_a, ps_b, ps_e, ps_f]

        s_ld = sem("s_ld")
        s_lg = sem("s_lg")
        s_lx = [sem("s_lx0"), sem("s_lx1")]
        s_q = sem("s_q")
        s_wl = [sem("s_wl0"), sem("s_wl1"), sem("s_wl2"), sem("s_wl3")]
        s_lx2 = [sem("s_lx20"), sem("s_lx21")]
        s_wo = [sem("s_wo0"), sem("s_wo1")]
        s_wol = sem("s_wol")
        s_od = [sem("s_od0"), sem("s_od1")]
        s_qc = sem("s_qc")
        s_es2 = [sem("s_es20"), sem("s_es21")]
        s_p2 = sem("s_p2")
        s_mm = sem("s_mm")
        s_pfb = [sem("s_pfb0"), sem("s_pfb1"), sem("s_pfb2"), sem("s_pfb3")]
        s_sq2 = sem("s_sq2")
        s_stp = sem("s_stp")
        s_r1 = sem("s_r1")
        s_r2 = sem("s_r2")
        s_r3 = sem("s_r3")
        s_r4 = sem("s_r4")
        s_r5 = sem("s_r5")
        s_bc = sem("s_bc")
        s_bce = sem("s_bce")
        s_apl = sem("s_apl")
        s_gel = sem("s_gel")
        s_ar = [sem("s_ar0"), sem("s_ar1")]
        s_bpr = sem("s_bpr")
        s_scn = sem("s_scn")
        s_scn2 = sem("s_scn2")
        s_vch = sem("s_vch")
        s_vch2 = sem("s_vch2")
        s_cc = sem("s_cc")
        s_cc2 = sem("s_cc2")
        s_g = sem("s_g")
        s_csc = sem("s_csc")
        s_sil = sem("s_sil")
        s_pou = sem("s_pou")
        s_woc = sem("s_woc")
        s_zm = sem("s_zm")
        s_ze = sem("s_ze")

        def a_bcast(c, n):
            return bass.AP(A_sb.ap().tensor, c, [[8, 128], [0, n]])

        def lbT_slice(w, mc):
            return bass.AP(lbt_sb.ap().tensor, w * D + mc * 128,
                           [[NW * D, 32], [1, 128]])

        def y1t(inst, c):
            if _SECOND.get(inst):
                t = qcbf if c < 4 else esbf
                return t[:, (c % 4) * TQ:(c % 4) * TQ + TQ]
            return y1bf[:, c * TQ:(c + 1) * TQ]

        with nc.Block() as blk:
            @blk.sync
            def _(sync):
                nld = [0]

                def d(out, in_):
                    sync.dma_start(out=out, in_=in_).then_inc(s_ld, 16)
                    nld[0] += 16
                d(bias_sb[:], bass.AP(bias_h, 0, [[8, 128], [1024, NW], [1, 8]]))
                d(lng_sb[:], bass.AP(lng_h, 0, [[8, 128], [1024, 3], [1, 8]]))
                d(lnb_sb[:], bass.AP(lnb_h, 0, [[8, 128], [1024, 3], [1, 8]]))
                d(va_sb[:], va_h.ap())
                d(vb_sb[:], vb_h.ap())
                d(vbin_sb[:], vbin_h.ap())
                d(vbout_sb[:], vbout_h.ap())
                d(m1_sb[:], m1_h.ap())
                d(mp_sb[:], mp_h.ap())
                d(mp2_sb[:], mp2_h.ap())
                d(no_sb[:], no33_c.ap())
                d(po_sb[:], po33_c.ap())
                d(bco_sb[:], bcones.ap())

                def arena_stream(w, mc):
                    cnt = _ar_cnt(w, mc)
                    if cnt >= 2:
                        sync.wait_ge(s_ar[mc % 2], 16 * (cnt - 1))
                    sync.dma_start(
                        out=xnbf[:, mc * 2048 + (w % 2) * 1024:
                                 mc * 2048 + (w % 2) * 1024 + 1024],
                        in_=bass.AP(w_hbm, w * 1048576 + mc * 131072,
                                    [[1024, 128], [1, 1024]]),
                    ).then_inc(s_ar[mc % 2], 16)

                for u in UNITS:
                    if u[0] == "W":
                        w = u[1]
                        for kc in range(CH):
                            i = w * 8 + kc
                            sync.wait_ge(s_p2, 2 * i + 2)
                            sync.dma_start(
                                out=bass.AP(w_hbm, w * 1048576 + kc * 128,
                                            [[1024, 128], [131072, 8], [1, 128]]),
                                in_=wchk[:, (i % 2) * 1024:(i % 2) * 1024 + 1024],
                            ).then_inc(s_wo[i % 2], 16)
                        if w == 3:
                            pos, bank, seq = _GINFO[("p", 1, 0, 1, 7)]
                            sync.wait_ge(s_pfb[bank], seq + 1)
                            sync.wait_ge(s_wo[0], 16 * 4 * 3)
                            sync.wait_ge(s_wo[1], 16 * 4 * 3)
                            for mc in range(CH):
                                arena_stream(2, mc)
                        if w == 5:
                            pos, bank, seq = _GINFO[("p", 3, 0, 1, 7)]
                            sync.wait_ge(s_pfb[bank], seq + 1)
                            sync.wait_ge(s_wo[0], 16 * 4 * 4)
                            sync.wait_ge(s_wo[1], 16 * 4 * 4)
                            for mc in range(CH):
                                arena_stream(3, mc)
                    elif u[0] == "B":
                        _, it, b = u
                        if b == 0:
                            for p in range(2):
                                w = p
                                for mc in range(CH):
                                    j = _stream_j0(it, p, mc)
                                    sync.wait_ge(s_wo[0], 16 * 4 * (w + 1))
                                    sync.wait_ge(s_wo[1], 16 * 4 * (w + 1))
                                    if j >= 4:
                                        sync.wait_ge(
                                            s_mm,
                                            _GINFO[_stream_key0(j - 4)][0] + 1)
                                    sync.dma_start(
                                        out=wstr[:, (j % 4) * 1024:
                                                 (j % 4) * 1024 + 1024],
                                        in_=bass.AP(w_hbm,
                                                    w * 1048576 + mc * 131072,
                                                    [[1024, 128], [1, 1024]]),
                                    ).then_inc(s_wl[j % 4], 16)
                    elif u[0] == "P" and u[1:] == (2, 3, 1):
                        if True:
                            for mc in range(CH):
                                sync.wait_ge(s_wo[0], 16 * 4 * 5)
                                sync.wait_ge(s_wo[1], 16 * 4 * 5)
                                sync.wait_ge(s_mm, _GINFO[("p", 3, 1, 0, mc)][0] + 1)
                                arena_stream(4, mc)
                            for mc in range(CH):
                                sync.wait_ge(s_wo[0], 16 * 4 * 6)
                                sync.wait_ge(s_wo[1], 16 * 4 * 6)
                                sync.wait_ge(s_mm, _GINFO[("p", 3, 1, 1, mc)][0] + 1)
                                arena_stream(5, mc)
                sync.wait_ge(s_od[0], 16 * 16)
                sync.wait_ge(s_od[1], 16 * 16)

            @blk.scalar
            def _(scalar):
                def rows_a(inst):
                    scalar.wait_ge(s_r1, inst + 1)
                    scalar.activation(mu2[:], nmu[:], AF.Square).then_inc(s_r2, 1)
                    scalar.wait_ge(s_r3, inst + 1)
                    scalar.activation(mu2[:], nmr[:], AF.Ln, bias=eps_sb[0:1, 0:1])
                    scalar.activation(rstd[:], mu2[:], AF.Exp,
                                      scale=-0.5).then_inc(s_r4, 1)

                def rows_b(inst):
                    scalar.wait_ge(s_bc, inst + 1)
                    scalar.wait_ge(s_apl, 8 * inst)
                    scalar.activation(c1bf[:], ps_c[:], AF.Copy)
                    scalar.activation(c0bf[:], ps_d[:], AF.Copy).then_inc(s_bce, 1)

                def rows_s(inst):
                    rows_a(inst)
                    rows_b(inst)

                def silus(it):
                    for mc in range(CH):
                        pos2, bank2, seq2 = _GINFO[("p", it, 2, 1, mc)]
                        k = it * 8 + mc
                        scalar.wait_ge(s_pfb[bank2], seq2 + 1)
                        if k >= 2:
                            scalar.wait_ge(s_pou, k - 1)
                        scalar.dma_start(
                            out=x2st[:, (k % 2) * TQ:(k % 2) * TQ + TQ],
                            in_=bass.AP(x_fm, mc * 128 * TOK + it * TQ,
                                        [[TOK, 128], [1, TQ]]),
                        ).then_inc(s_lx2[k % 2], 16)
                        if k >= 2:
                            kp = k - 2
                            scalar.dma_start(
                                out=bass.AP(out_fm,
                                            (kp % 8) * 128 * TOK + (kp // 8) * TQ,
                                            [[TOK, 128], [1, TQ]]),
                                in_=outt[:, (kp % 2) * TQ:(kp % 2) * TQ + TQ],
                            ).then_inc(s_od[kp % 2], 16)
                        scalar.activation(gbf[:, (mc % 2) * TQ:(mc % 2) * TQ + TQ],
                                          Sst[:, it * 4096 + mc * TQ:
                                              it * 4096 + mc * TQ + TQ],
                                          AF.Silu).then_inc(s_sil, 1)

                scalar.wait_ge(s_ld, 13 * 16)
                scalar.activation(A_sb[:], va_sb[:], AF.Sigmoid).then_inc(s_wol, 1)
                bufb_open = [False]
                for u in UNITS:
                    if u[0] == "W":
                        continue
                    elif u[0] == "A":
                        tq = u[1]
                        inst = _inst_a(tq)
                        rows_s(inst)
                        if tq == 0:
                            scalar.wait_ge(s_bpr, 1)
                        for mc in range(CH):
                            pos, bank, seq = _GINFO[("h", tq, mc)]
                            scalar.wait_ge(s_mm, pos + 1)
                            scalar.activation(
                                bhbf[:, mc * TOK + tq * TQ: mc * TOK + tq * TQ + TQ],
                                ps_p4[bank][:], AF.Identity,
                                bias=bprod_sb[:, mc:mc + 1],
                                scale=vb_sb[:, mc:mc + 1],
                            ).then_inc(s_pfb[bank], 1)
                    elif u[0] == "B" or u[0] == "P":
                        def zev(it, b, p):
                            zi = _ZINFO[(it, b, p)]
                            scalar.wait_ge(s_zm, zi + 1)
                            if zi >= 1:
                                pk = _ZREV[zi - 1]
                                scalar.wait_ge(
                                    s_mm,
                                    _GINFO[("p", pk[0], pk[1], pk[2], 7)][0] + 1)
                            scalar.activation(zbf[:], ps_z[:],
                                              AF.Copy).then_inc(s_ze, 1)

                        def evacs_rows(it, b, inst, defer_bc=False):
                            if _SECOND.get(inst) and not bufb_open[0]:
                                scalar.wait_ge(s_p2, 96)
                                bufb_open[0] = True
                            for mc in range(CH):
                                pos, bank, seq = _GINFO[("p", it, b, 0, mc)]
                                scalar.wait_ge(s_mm, pos + 1)
                                scalar.activation(
                                    y1t(inst, mc),
                                    ps_p4[bank][:], AF.Identity,
                                    bias=bias_sb[:, 2 * b * 8 + mc:
                                                 2 * b * 8 + mc + 1],
                                ).then_inc(s_pfb[bank], 1)
                            rows_a(inst)
                            if not defer_bc:
                                rows_b(inst)

                        def gelus(it, b, inst):
                            for mc in range(CH):
                                k = inst * 8 + mc
                                scalar.wait_ge(s_apl, k + 1)
                                scalar.wait_ge(s_stp, k + 1)
                                scalar.activation(
                                    y1t(inst, mc),
                                    t2bf[:, (mc % 4) * TQ:(mc % 4) * TQ + TQ],
                                    AF.Gelu,
                                    scale=lng_sb[:, b * 8 + mc:b * 8 + mc + 1],
                                    bias=lnb_sb[:, b * 8 + mc:b * 8 + mc + 1],
                                ).then_inc(s_gel, 1)

                        if u[0] == "B":
                            _, it, b = u
                            inst = _inst_b(it, b)
                            zev(it, b, 0)
                            evacs_rows(it, b, inst)
                            gelus(it, b, inst)
                            zev(it, b, 1)
                            if b == 2:
                                silus(it)
                        else:
                            _, itX, itY, b = u
                            iX, iY = _inst_b(itX, b), _inst_b(itY, b)
                            zev(itX, b, 0)
                            evacs_rows(itX, b, iX)
                            zev(itY, b, 0)
                            gelus(itX, b, iX)
                            evacs_rows(itY, b, iY, defer_bc=True)
                            zev(itX, b, 1)
                            rows_b(iY)
                            gelus(itY, b, iY)
                            zev(itY, b, 1)
                            if b == 2:
                                silus(itX)
                                silus(itY)
                                if itY == 3:
                                    for kp in (30, 31):
                                        scalar.wait_ge(s_pou, kp + 1)
                                        scalar.dma_start(
                                            out=bass.AP(out_fm,
                                                        (kp % 8) * 128 * TOK
                                                        + (kp // 8) * TQ,
                                                        [[TOK, 128], [1, TQ]]),
                                            in_=outt[:, (kp % 2) * TQ:
                                                     (kp % 2) * TQ + TQ],
                                        ).then_inc(s_od[kp % 2], 16)

            @blk.gpsimd
            def _(gpsimd):
                gp_units = [("A", 0), ("A", 1), ("W", 0), ("W", 1),
                            ("AL", 2), ("AL", 3), ("W", 2), ("W", 3),
                            ("W", 4), ("W", 5), ("AX", 3)]
                qsem = [s_q, s_qc]

                def qload(w, kc):
                    g = w * 8 + kc
                    gpsimd.dma_start(
                        out=qbf[:, (kc % 2) * 1024:(kc % 2) * 1024 + 1024],
                        in_=bass.AP(q_t, w * D * D + kc * 128 * D,
                                    [[D, 128], [1, D]]),
                    ).then_inc(qsem[g % 2], 16)

                def esload(w, kc):
                    g = w * 8 + kc
                    gpsimd.dma_start(
                        out=esbf[:, (g % 2) * 1024:(g % 2) * 1024 + 1024],
                        in_=bass.AP(sc_t, w * 64 * D + kc * 8 * D,
                                    [[D, 8], [0, 16], [1, D]]),
                    ).then_inc(s_es2[g % 2], 16)

                for u in gp_units:
                    if u[0] == "W":
                        w = u[1]
                        if w == 0:
                            gpsimd.dma_start(
                                out=wio[:],
                                in_=bass.AP(wint_h, 0,
                                            [[D, 128], [128 * D, CH], [1, D]]),
                            ).then_inc(s_woc, 16)
                        qload(w, 0)
                        qload(w, 1)
                        esload(w, 0)
                        for kc in range(CH):
                            i = w * 8 + kc
                            hfq = (i % 2) * 1024
                            gpsimd.wait_ge(qsem[i % 2], 16 * (i // 2 + 1))
                            gpsimd.tensor_scalar(
                                qcbf[:, hfq:hfq + 1024],
                                qbf[:, hfq:hfq + 1024],
                                1.0 / 7.5, -1.0, OP.mult, OP.add)
                            if kc + 2 < CH:
                                qload(w, kc + 2)
                            if kc + 1 < CH:
                                esload(w, kc + 1)
                            gpsimd.wait_ge(s_es2[i % 2], 16 * (i // 2 + 1))
                            if i >= 2:
                                gpsimd.wait_ge(s_wo[i % 2], 16 * (i // 2))
                            gpsimd.tensor_mul(
                                wchk[:, hfq:hfq + 1024],
                                qcbf[:, hfq:hfq + 1024],
                                esbf[:, hfq:hfq + 1024],
                            ).then_inc(s_p2, 2)
                    elif u[0] in ("A", "AL"):
                        tq = u[1]
                        xdst = xbf if tq % 2 == 0 else ybf
                        for p in range(4):
                            kp = tq * 4 + p
                            if tq >= 2:
                                gpsimd.wait_ge(s_apl,
                                               8 * _inst_a(tq - 2) + 2 * (p + 1))
                            if kp >= 2:
                                gpsimd.wait_ge(s_lx[kp % 2], 16 * (kp // 2))
                            gpsimd.dma_start(
                                out=xdst[:, 2 * p * TQ:2 * p * TQ + 2 * TQ],
                                in_=bass.AP(x_fm, 2 * p * 128 * TOK + tq * TQ,
                                            [[TOK, 128], [128 * TOK, 2], [1, TQ]]),
                            ).then_inc(s_lx[kp % 2], 16)
                        if tq == 1:
                            for w in range(NW):
                                gpsimd.dma_start(
                                    out=latall[:, w * 256:(w + 1) * 256],
                                    in_=bass.AP(lat_h, w * D * 32,
                                                [[32, 128], [128 * 32, 8],
                                                 [1, 32]]),
                                ).then_inc(s_lg, 16)
                            gpsimd.dma_start(out=lbt_sb[:],
                                             in_=lbt_h.ap()).then_inc(s_lg, 16)
                    elif u[0] == "AX":
                        if True:
                            gpsimd.wait_ge(s_mm, _GINFO[("h", 3, 7)][0] + 1)
                            gpsimd.dma_start(
                                out=wio[:],
                                in_=bass.AP(woutt_h, 0,
                                            [[D, 128], [128 * D, CH], [1, D]]),
                            ).then_inc(s_woc, 16)
                            gpsimd.wait_ge(s_scn2, 32)
                            gpsimd.dma_start(out=f_bnc.ap(),
                                             in_=stA[:, 24:32]).then_inc(s_cc, 16)
                            gpsimd.wait_ge(s_cc, 16)
                            gpsimd.collective_compute(
                                "AllGather", OP.bypass,
                                replica_groups=[list(range(NCORE))],
                                ins=[f_bnc.ap().opt()],
                                outs=[g_bnc.ap().opt()],
                            ).then_inc(s_cc2, 1)
                            gpsimd.wait_ge(s_cc2, 1)
                            gpsimd.dma_start(
                                out=gg_sb[:],
                                in_=bass.AP(g_bnc, 0, [[8, 128], [1024, 8], [1, 8]]),
                            ).then_inc(s_g, 16)

            @blk.tensor
            def _(tensor):
                tensor.wait_ge(s_ld, 13 * 16)

                def proj_group(key, lhsT_tile, rhs_tiles, lora_w=None):
                    pos, bank, seq = _GINFO[key]
                    if seq >= 1:
                        tensor.wait_ge(s_pfb[bank], seq)
                    last = 7 if lora_w is None else 8
                    for kc in range(CH):
                        mm = tensor.matmul(
                            ps_p4[bank][:], lhsT_tile(kc), rhs_tiles(kc),
                            start=(kc == 0), stop=(kc == last))
                    if lora_w is not None:
                        w, mc = lora_w
                        mm = tensor.matmul(ps_p4[bank][:], lbT_slice(w, mc), zbf[:],
                                           start=False, stop=True)
                    mm.then_inc(s_mm, 1)

                def zmms(it, b, p, rhs_tiles):
                    zi = _ZINFO[(it, b, p)]
                    w = 2 * b + p
                    if zi >= 1:
                        tensor.wait_ge(s_ze, zi)
                    for kc in range(CH):
                        mm = tensor.matmul(
                            ps_z[:],
                            latall[:, w * 256 + kc * 32: w * 256 + kc * 32 + 32],
                            rhs_tiles(kc),
                            start=(kc == 0), stop=(kc == 7))
                    mm.then_inc(s_zm, 1)

                def stats_pair(inst, c, src_bf):
                    k = inst * 8 + c
                    tensor.wait_ge(s_sq2, k + 1)
                    tensor.matmul(ps_st[:], no_sb[:], src_bf,
                                  start=(c == 0), stop=False)
                    mm = tensor.matmul(ps_st[:], po_sb[:],
                                       sqs[:, (c % 2) * TQ:(c % 2) * TQ + TQ],
                                       start=False, stop=(c == 7))
                    mm.then_inc(s_stp, 1)

                def bcast_inst(inst):
                    tensor.wait_ge(s_r4, inst + 1)
                    tensor.wait_ge(s_r5, inst + 1)
                    tensor.wait_ge(s_bce, inst)
                    tensor.matmul(ps_c[:], bco_sb[:], rstd[:], start=True, stop=True)
                    tensor.matmul(ps_d[:], bco_sb[:], nmr[:],
                                  start=True, stop=True).then_inc(s_bc, 1)

                def sproj(it):
                    tensor.wait_ge(s_woc, 32)
                    tensor.wait_ge(s_csc, 8 * (it + 1))
                    for mc in range(CH):
                        proj_group(
                            ("s", it, mc),
                            lambda kc, mc=mc: wio[:, kc * D + mc * 128:
                                                  kc * D + mc * 128 + 128],
                            lambda kc: ybf[:, kc * TQ:(kc + 1) * TQ])

                def wtile(it, b, p, mc):
                    w = 2 * b + p
                    if b == 0:
                        j = _stream_j0(it, p, mc)
                        tensor.wait_ge(s_wl[j % 4], 16 * (j // 4 + 1))
                        return lambda kc, j=j: wstr[
                            :, (j % 4) * 1024 + kc * 128:
                            (j % 4) * 1024 + kc * 128 + 128]
                    tensor.wait_ge(s_ar[mc % 2], 16 * (_ar_cnt(w, mc) + 1))
                    base = mc * 2048 + (w % 2) * 1024
                    return lambda kc, base=base: xnbf[
                        :, base + kc * 128: base + kc * 128 + 128]

                def t_first(it, b, inst, do_bcast=True):
                    if b == 0:
                        rhs = lambda kc: xnbf[:, kc * TOK + it * TQ:
                                              kc * TOK + it * TQ + TQ]
                    else:
                        need = {}
                        for mc in range(CH):
                            pos, bank, seq = _GINFO[("p", it, b - 1, 1, mc)]
                            need[bank] = max(need.get(bank, -1), seq)
                        for bank in sorted(need):
                            tensor.wait_ge(s_pfb[bank], need[bank] + 1)
                        rhs = lambda kc: Sst[:, it * 4096 + kc * TQ:
                                             it * 4096 + kc * TQ + TQ]
                    tensor.wait_ge(s_r1, inst)
                    if (it, b) == (0, 0):
                        tensor.wait_ge(s_lg, 112)
                    zmms(it, b, 0, rhs)
                    zi1 = _ZINFO[(it, b, 0)]
                    for mc in range(CH):
                        lt = wtile(it, b, 0, mc)
                        if mc == 0:
                            tensor.wait_ge(s_ze, zi1 + 1)
                        proj_group(("p", it, b, 0, mc), lt, rhs,
                                   lora_w=(2 * b, mc))
                        if mc >= 2:
                            stats_pair(inst, mc - 2, y1t(inst, mc - 2))
                    for c in (6, 7):
                        stats_pair(inst, c, y1t(inst, c))
                    if do_bcast:
                        bcast_inst(inst)

                def t_second(it, b, inst):
                    tensor.wait_ge(s_gel, 8 * (_GPRE[inst] + 1))
                    zmms(it, b, 1, lambda kc: y1t(inst, kc))
                    zi2 = _ZINFO[(it, b, 1)]
                    for mc in range(CH):
                        lt = wtile(it, b, 1, mc)
                        if mc == 0:
                            tensor.wait_ge(s_ze, zi2 + 1)
                        proj_group(("p", it, b, 1, mc), lt,
                                   lambda kc: y1t(inst, kc),
                                   lora_w=(2 * b + 1, mc))

                for u in UNITS:
                    if u[0] == "W":
                        continue
                    if u[0] == "A":
                        tq = u[1]
                        xsrc = xbf if tq % 2 == 0 else ybf
                        inst = _inst_a(tq)
                        tensor.wait_ge(s_r1, inst)
                        for c in range(CH):
                            stats_pair(inst, c, xsrc[:, c * TQ:(c + 1) * TQ])
                        bcast_inst(inst)
                        tensor.wait_ge(s_apl, 8 * (inst + 1))
                        if tq == 0:
                            tensor.wait_ge(s_woc, 16)
                        for mc in range(CH):
                            proj_group(
                                ("h", tq, mc),
                                lambda kc, mc=mc: wio[:, kc * D + mc * 128:
                                                      kc * D + mc * 128 + 128],
                                lambda kc: xnbf[:, kc * TOK + tq * TQ:
                                                kc * TOK + tq * TQ + TQ])
                    elif u[0] == "B":
                        _, it, b = u
                        inst = _inst_b(it, b)
                        t_first(it, b, inst)
                        t_second(it, b, inst)
                    else:
                        _, itX, itY, b = u
                        iX, iY = _inst_b(itX, b), _inst_b(itY, b)
                        t_first(itX, b, iX)
                        t_first(itY, b, iY)
                        t_second(itX, b, iX)
                        t_second(itY, b, iY)
                        if b == 2:
                            sproj(itX)
                            sproj(itY)

            @blk.vector
            def _(vector):
                vc = [0]

                def g(instr):
                    instr.then_inc(s_vch, 1)
                    vc[0] += 1
                    vector.wait_ge(s_vch, vc[0])

                g(vector.memset(eps_sb[:], EPS))
                g(vector.memset(neg1_sb[:], -1.0))
                vector.wait_ge(s_wol, 1)
                bufs = [ptmp, ptmp2]
                g(vector.tensor_mul(ptmp[:], A_sb[:], A_sb[:]))
                cur = ptmp
                for i in range(8):
                    dst = a512 if i == 7 else bufs[(i + 1) % 2]
                    g(vector.tensor_mul(dst[:], cur[:], cur[:]))
                    cur = dst
                g(vector.tensor_mul(a1024[:], a512[:], a512[:]))
                g(vector.tensor_mul(a1536[:], a1024[:], a512[:]))
                g(vector.tensor_mul(p2048[:], a1024[:], a1024[:]))
                g(vector.tensor_mul(p4096[:], p2048[:], p2048[:]))
                vector.tensor_mul(bprod_sb[:], vb_sb[:],
                                  vbin_sb[:]).then_inc(s_bpr, 1)
                g(vector.tensor_copy(
                    crep[:], bass.AP(p2048.ap().tensor, 0, [[8, 128], [0, 8], [1, 8]])))
                g(vector.tensor_mul(u_sb[:], mp_sb[:], crep[:]))
                g(vector.tensor_add(gg_sb[:], m1_sb[:], u_sb[:]))
                g(vector.tensor_copy(
                    crep[:], bass.AP(p4096.ap().tensor, 0, [[8, 128], [0, 8], [1, 8]])))
                g(vector.tensor_mul(u_sb[:], mp2_sb[:], crep[:]))
                g(vector.tensor_add(coef[:], gg_sb[:], u_sb[:]))

                def rows_inst(inst):
                    vector.wait_ge(s_stp, 8 * (inst + 1))
                    vector.tensor_copy(nmu[:], ps_st[0:1, :])
                    vector.tensor_copy(msq[:], ps_st[32:33, :]).then_inc(s_r1, 1)
                    vector.wait_ge(s_r2, inst + 1)
                    vector.tensor_sub(nmr[:], msq[:], mu2[:]).then_inc(s_r3, 1)
                    vector.wait_ge(s_r4, inst + 1)
                    vector.tensor_mul(nmr[:], nmu[:], rstd[:]).then_inc(s_r5, 1)

                def apply_inst(inst, src_bf, dst, gel_guard=False):
                    vector.wait_ge(s_bce, inst + 1)
                    for c in range(CH):
                        if gel_guard:
                            g = 8 * _GPRE[inst] + c - 3
                            if g > 0:
                                vector.wait_ge(s_gel, g)
                        vector.tensor_mul(t1bf[:], src_bf(c), c1bf[:])
                        vector.tensor_add(dst(c), t1bf[:], c0bf[:]).then_inc(s_apl, 1)

                def wait_evac(key):
                    pos, bank, seq = _GINFO[key]
                    vector.wait_ge(s_pfb[bank], seq + 1)

                def pstt_out(it):
                    for mc in range(CH):
                        pos, bank, seq = _GINFO[("s", it, mc)]
                        k = it * 8 + mc
                        vector.wait_ge(s_mm, pos + 1)
                        vector.wait_ge(s_sil, k + 1)
                        vector.scalar_tensor_tensor(
                            tpbf[:], ps_p4[bank][:],
                            vbout_sb[:, mc:mc + 1],
                            gbf[:, (mc % 2) * TQ:(mc % 2) * TQ + TQ],
                            OP.add, OP.mult,
                        ).then_inc(s_pfb[bank], 1)
                        vector.wait_ge(s_lx2[k % 2], 16 * (k // 2 + 1))
                        if k >= 2:
                            vector.wait_ge(s_od[k % 2], 16 * (k // 2))
                        vector.tensor_add(outt[:, (k % 2) * TQ:(k % 2) * TQ + TQ],
                                          x2st[:, (k % 2) * TQ:(k % 2) * TQ + TQ],
                                          tpbf[:]).then_inc(s_pou, 1)

                def cscans(it):
                    if it > 0:
                        vector.wait_ge(s_mm, _GINFO[("s", it - 1, 7)][0] + 1)
                    for c in range(CH):
                        init = (carry[:, c:c + 1] if it == 0
                                else stC[:, it * 8 + c: it * 8 + c + 1])
                        vector.tensor_tensor_scan(
                            ybf[:, c * TQ:(c + 1) * TQ], a_bcast(c, TQ),
                            bhbf[:, c * TOK + it * TQ: c * TOK + it * TQ + TQ],
                            init, OP.mult, OP.add).then_inc(s_csc, 1)

                for u in UNITS:
                    if u[0] == "W":
                        continue
                    elif u[0] == "A":
                        tq = u[1]
                        xsrc = xbf if tq % 2 == 0 else ybf
                        inst = _inst_a(tq)
                        for c in range(CH):
                            k = inst * 8 + c
                            kp = tq * 4 + c // 2
                            vector.wait_ge(s_lx[kp % 2], 16 * (kp // 2 + 1))
                            if k >= 2:
                                vector.wait_ge(s_stp, k - 1)
                            vector.tensor_mul(sqs[:, (c % 2) * TQ:(c % 2) * TQ + TQ],
                                              xsrc[:, c * TQ:(c + 1) * TQ],
                                              xsrc[:, c * TQ:(c + 1) * TQ]).then_inc(s_sq2, 1)
                        rows_inst(inst)
                        apply_inst(
                            inst,
                            lambda c: xsrc[:, c * TQ:(c + 1) * TQ],
                            lambda c: xnbf[:, c * TOK + tq * TQ: c * TOK + tq * TQ + TQ])
                        for c in range(CH):
                            wait_evac(("h", tq, c))
                            init = 0.0 if tq == 0 else stA[:, (tq - 1) * 8 + c:
                                                           (tq - 1) * 8 + c + 1]
                            vector.tensor_tensor_scan(
                                tpbf[:], a_bcast(c, TQ),
                                bhbf[:, c * TOK + tq * TQ: c * TOK + tq * TQ + TQ],
                                init, OP.mult, OP.add).then_inc(s_scn, 1)
                            vector.wait_ge(s_scn, tq * 8 + c + 1)
                            vector.tensor_copy(stA[:, tq * 8 + c: tq * 8 + c + 1],
                                               tpbf[:, TQ - 1:TQ]).then_inc(s_scn2, 1)
                    elif u[0] == "B" or u[0] == "P":
                        def v_first(it, b, inst):
                            for mc in range(CH):
                                k = inst * 8 + mc
                                wait_evac(("p", it, b, 0, mc))
                                if k >= 2:
                                    vector.wait_ge(s_stp, k - 1)
                                vector.tensor_mul(
                                    sqs[:, (mc % 2) * TQ:(mc % 2) * TQ + TQ],
                                    y1t(inst, mc),
                                    y1t(inst, mc)).then_inc(s_sq2, 1)
                            rows_inst(inst)
                            apply_inst(inst,
                                       lambda c: y1t(inst, c),
                                       lambda c: t2bf[:, (c % 4) * TQ:
                                                      (c % 4) * TQ + TQ],
                                       gel_guard=True)
                        def v_second(it, b, inst):
                            for mc in range(CH):
                                pos, bank, seq = _GINFO[("p", it, b, 1, mc)]
                                vector.wait_ge(s_mm, pos + 1)
                                sdst = Sst[:, it * 4096 + mc * TQ:
                                           it * 4096 + mc * TQ + TQ]
                                if b == 0:
                                    res_in = xnbf[:, mc * TOK + it * TQ:
                                                  mc * TOK + it * TQ + TQ]
                                else:
                                    res_in = sdst
                                vector.scalar_tensor_tensor(
                                    sdst, ps_p4[bank][:],
                                    bias_sb[:, (2 * b + 1) * 8 + mc:
                                            (2 * b + 1) * 8 + mc + 1],
                                    res_in, OP.add, OP.add,
                                ).then_inc(s_pfb[bank], 1)

                        if u[0] == "B":
                            _, it, b = u
                            inst = _inst_b(it, b)
                            v_first(it, b, inst)
                            v_second(it, b, inst)
                        else:
                            _, itX, itY, b = u
                            iX, iY = _inst_b(itX, b), _inst_b(itY, b)
                            v_first(itX, b, iX)
                            v_first(itY, b, iY)
                            v_second(itX, b, iX)
                            v_second(itY, b, iY)
                            if b == 2:
                                pstt_out(itX)
                                if itX + 1 < 4:
                                    cscans(itX + 1)
                                pstt_out(itY)
                                if itY + 1 < 4:
                                    cscans(itY + 1)
                        if u[0] == "P" and u[1:] == (0, 1, 1):
                            vector.wait_ge(s_g, 16)
                            vcc = [0]

                            def gc(instr):
                                instr.then_inc(s_vch2, 1)
                                vcc[0] += 1
                                vector.wait_ge(s_vch2, vcc[0])

                            gc(vector.tensor_mul(u_sb[:], gg_sb[:], coef[:]))
                            cbufs = [carry, carry2]
                            gc(vector.tensor_add(carry[:], u_sb[:, 0:8],
                                                 u_sb[:, 8:16]))
                            cur = 0
                            for j in range(2, 8):
                                gc(vector.tensor_add(cbufs[(cur + 1) % 2][:],
                                                     cbufs[cur][:],
                                                     u_sb[:, j * 8:(j + 1) * 8]))
                                cur = (cur + 1) % 2
                            if cur != 0:
                                gc(vector.tensor_copy(carry[:], carry2[:]))
                            for tq2, ap in ((1, a512), (2, a1024), (3, a1536)):
                                gc(vector.tensor_mul(ptmp[:], ap[:], carry[:]))
                                gc(vector.tensor_add(stC[:, tq2 * 8:(tq2 + 1) * 8],
                                                     stA[:, (tq2 - 1) * 8:tq2 * 8],
                                                     ptmp[:]))
                            cscans(0)

    return nc


_NC_CACHE = {}


def _get_nc():
    if "nc" not in _NC_CACHE:
        _NC_CACHE["nc"] = build_nc()
    return _NC_CACHE["nc"]


def _vec128(v):
    return np.ascontiguousarray(np.asarray(v, dtype=np.float32).reshape(8, 128).T)


def kernel(L0_attn_x, L0_attn_w_in, L0_attn_b_in, L0_attn_a, L0_attn_b,
           L0_attn_w_out, L0_attn_b_out, L0_ffn_qweights, L0_ffn_scales,
           L0_ffn_biases, L0_ffn_lora_a, L0_ffn_lora_b, L0_ffn_ln_gamma,
           L0_ffn_ln_beta):
    x = np.asarray(L0_attn_x, dtype=np.float32).reshape(16384, D)
    q_t = np.ascontiguousarray(np.asarray(L0_ffn_qweights, dtype=np.int32)
                               .transpose(0, 2, 1))
    sc_t = np.ascontiguousarray(np.asarray(L0_ffn_scales, dtype=np.float32)
                                .transpose(0, 2, 1))
    la = np.asarray(L0_ffn_lora_a, dtype=np.float32)
    lat = np.ascontiguousarray(la.transpose(0, 2, 1))
    lbt = np.asarray(L0_ffn_lora_b, dtype=np.float32).transpose(0, 2, 1)
    lbt_all = np.ascontiguousarray(lbt.transpose(1, 0, 2).reshape(32, NW * D))
    biases = np.ascontiguousarray(
        np.asarray(L0_ffn_biases, dtype=np.float32).reshape(NW, 8, 128)
        .transpose(0, 2, 1))
    lng = np.ascontiguousarray(
        np.asarray(L0_ffn_ln_gamma, dtype=np.float32).reshape(3, 8, 128)
        .transpose(0, 2, 1))
    lnb = np.ascontiguousarray(
        np.asarray(L0_ffn_ln_beta, dtype=np.float32).reshape(3, 8, 128)
        .transpose(0, 2, 1))
    va = _vec128(L0_attn_a)
    vb = _vec128(L0_attn_b)
    vbin = _vec128(L0_attn_b_in)
    vbout = _vec128(L0_attn_b_out)
    wint = np.ascontiguousarray(np.asarray(L0_attn_w_in, dtype=np.float32).T)
    woutt = np.ascontiguousarray(np.asarray(L0_attn_w_out, dtype=np.float32).T)

    in_maps = []
    for c in range(NCORE):
        k = c % 4
        m1 = np.zeros((128, 64), np.float32)
        mp = np.zeros((128, 64), np.float32)
        mp2 = np.zeros((128, 64), np.float32)
        base = (c // 4) * 4
        if k >= 1:
            m1[:, (base + k - 1) * 8:(base + k) * 8] = 1.0
        if k >= 2:
            mp[:, (base + k - 2) * 8:(base + k - 1) * 8] = 1.0
        if k >= 3:
            mp2[:, (base + k - 3) * 8:(base + k - 2) * 8] = 1.0
        x_fmc = np.ascontiguousarray(x[c * TOK:(c + 1) * TOK, :].T)
        in_maps.append(dict(
            x_fm=x_fmc, q_t=q_t, sc_t=sc_t, lat_h=lat, lbt_h=lbt_all,
            bias_h=biases, lng_h=lng, lnb_h=lnb,
            va_h=va, vb_h=vb, vbin_h=vbin, vbout_h=vbout,
            wint_h=wint, woutt_h=woutt,
            m1_h=m1, mp_h=mp, mp2_h=mp2,
        ))

    nc = _get_nc()
    res = run_bass_kernel_spmd(nc, in_maps, core_ids=list(range(NCORE)))
    out = np.empty((16384, D), dtype=np.float32)
    for c in range(NCORE):
        out[c * TOK:(c + 1) * TOK, :] = res.results[c]["out_fm"].T
    return out.reshape(2, 8192, D)


# version 23
# speedup vs baseline: 1.1023x; 1.0155x over previous
import contextlib
import numpy as np
import ml_dtypes
import concourse.bass as bass
import concourse.mybir as mybir
from concourse.bass_utils import run_bass_kernel_spmd

F32 = mybir.dt.float32
BF16 = mybir.dt.bfloat16
I32 = mybir.dt.int32
AF = mybir.ActivationFunctionType
OP = mybir.AluOpType

NCORE = 8
TOK = 2048
D = 1024
CH = 8
TQ = 512
NTQ = 4
NW = 6
EPS = 1e-5
BF = ml_dtypes.bfloat16

UNITS = [("W", 0), ("W", 1), ("A", 0), ("A", 1), ("B", 0, 0), ("W", 2),
         ("A", 2), ("B", 1, 0), ("W", 3), ("A", 3), ("B", 2, 0), ("W", 4),
         ("B", 3, 0), ("W", 5),
         ("P", 0, 1, 1), ("P", 2, 3, 1), ("P", 0, 1, 2), ("P", 2, 3, 2)]


def _build_tables():
    ginfo = {}
    inst_of = {}
    gelu_pre = {}
    zinfo = {}
    second = {}
    order = []
    ninst = 0
    nb = 0
    nz = 0
    for u in UNITS:
        if u[0] == "W":
            continue
        if u[0] == "A":
            tq = u[1]
            inst_of[u] = ninst
            gelu_pre[ninst] = nb
            ninst += 1
            for mc in range(8):
                order.append(("h", tq, mc))
        elif u[0] == "B":
            _, it, b = u
            inst_of[u] = ninst
            gelu_pre[ninst] = nb
            ninst += 1
            nb += 1
            zinfo[(it, b, 0)] = nz
            nz += 1
            for mc in range(8):
                order.append(("p", it, b, 0, mc))
            zinfo[(it, b, 1)] = nz
            nz += 1
            for mc in range(8):
                order.append(("p", it, b, 1, mc))
        else:
            _, itX, itY, b = u
            for idx, it in enumerate((itX, itY)):
                inst_of[("B", it, b)] = ninst
                gelu_pre[ninst] = nb
                if idx == 1:
                    second[ninst] = True
                ninst += 1
                nb += 1
            for it in (itX, itY):
                zinfo[(it, b, 0)] = nz
                nz += 1
            for it in (itX, itY):
                for mc in range(8):
                    order.append(("p", it, b, 0, mc))
            for it in (itX, itY):
                zinfo[(it, b, 1)] = nz
                nz += 1
            for it in (itX, itY):
                for mc in range(8):
                    order.append(("p", it, b, 1, mc))
            if b == 2:
                for it in (itX, itY):
                    for mc in range(8):
                        order.append(("s", it, mc))
    ginfo_banks = [0, 0, 0, 0]
    for pos, key in enumerate(order):
        bank = pos % 4
        ginfo[key] = (pos, bank, ginfo_banks[bank])
        ginfo_banks[bank] += 1
    return ginfo, inst_of, gelu_pre, zinfo, second


_GINFO, _INST, _GPRE, _ZINFO, _SECOND = _build_tables()
_ZREV = {v: k for k, v in _ZINFO.items()}


def _inst_a(tq):
    return _INST[("A", tq)]


def _inst_b(it, b):
    return _INST[("B", it, b)]


def _stream_j0(it, p, mc):
    return it * 16 + p * 8 + mc


def _stream_key0(j):
    return ("p", j // 16, 0, (j % 16) // 8, j % 8)


def _ar_cnt(w, mc):
    return (w - 2) * 4 + mc // 2


def build_nc():
    nc = bass.Bass("TRN2", target_bir_lowering=False, detect_race_conditions=False)

    x_fm = nc.dram_tensor("x_fm", [D, TOK], F32, kind="ExternalInput")
    q_t = nc.dram_tensor("q_t", [NW, D, D], I32, kind="ExternalInput")
    sc_t = nc.dram_tensor("sc_t", [NW, 64, D], F32, kind="ExternalInput")
    lat_h = nc.dram_tensor("lat_h", [NW, D, 32], F32, kind="ExternalInput")
    lbt_h = nc.dram_tensor("lbt_h", [32, NW * D], F32, kind="ExternalInput")
    bias_h = nc.dram_tensor("bias_h", [NW, 128, 8], F32, kind="ExternalInput")
    lng_h = nc.dram_tensor("lng_h", [3, 128, 8], F32, kind="ExternalInput")
    lnb_h = nc.dram_tensor("lnb_h", [3, 128, 8], F32, kind="ExternalInput")
    va_h = nc.dram_tensor("va_h", [128, 8], F32, kind="ExternalInput")
    vb_h = nc.dram_tensor("vb_h", [128, 8], F32, kind="ExternalInput")
    vbin_h = nc.dram_tensor("vbin_h", [128, 8], F32, kind="ExternalInput")
    vbout_h = nc.dram_tensor("vbout_h", [128, 8], F32, kind="ExternalInput")
    wint_h = nc.dram_tensor("wint_h", [D, D], F32, kind="ExternalInput")
    woutt_h = nc.dram_tensor("woutt_h", [D, D], F32, kind="ExternalInput")
    m1_h = nc.dram_tensor("m1_h", [128, 64], F32, kind="ExternalInput")
    mp_h = nc.dram_tensor("mp_h", [128, 64], F32, kind="ExternalInput")
    mp2_h = nc.dram_tensor("mp2_h", [128, 64], F32, kind="ExternalInput")
    out_fm = nc.dram_tensor("out_fm", [D, TOK], F32, kind="ExternalOutput")

    w_hbm = nc.dram_tensor("w_hbm", [NW * 8 * 128 * 8 * 128], BF16)
    f_bnc = nc.dram_tensor("f_bnc", [128, 8], F32)
    g_bnc = nc.dram_tensor("g_bnc", [128 * NCORE, 8], F32)

    no33 = np.zeros((128, 33), dtype=BF)
    no33[:, 0] = -1.0 / D
    po33 = np.zeros((128, 33), dtype=BF)
    po33[:, 32] = 1.0 / D
    no33_c = nc.inline_tensor(no33, "no33c")
    po33_c = nc.inline_tensor(po33, "po33c")
    bcones = nc.inline_tensor(np.ones((1, 128), dtype=BF), "bconesc")

    es = contextlib.ExitStack()
    with es:
        sb = lambda name, shape, dt: es.enter_context(nc.sbuf_tensor(name, shape, dt))
        psb = lambda name, shape: es.enter_context(nc.psum_tensor(name, shape, F32))
        sem = lambda name: es.enter_context(nc.semaphore(name))

        xnbf = sb("xnbf", [128, CH * TOK], BF16)
        bhbf = sb("bhbf", [128, CH * TOK], BF16)
        Sst = sb("Sst", [128, NTQ * CH * TQ], BF16)
        wio = sb("wio", [128, CH * D], BF16)
        wstr = sb("wstr", [128, 4 * 1024], BF16)
        qbf = sb("qbf", [128, 2 * 1024], BF16)
        qcbf = sb("qcbf", [128, 2 * 1024], BF16)
        esbf = sb("esbf", [128, 2 * 1024], BF16)
        wchk = sb("wchk", [128, 2 * 1024], BF16)
        latall = sb("latall", [128, NW * 256], BF16)
        lbt_sb = sb("lbt_sb", [32, NW * D], BF16)
        zbf = sb("zbf", [32, TQ], BF16)
        bias_sb = sb("bias_sb", [128, NW * 8], F32)
        lng_sb = sb("lng_sb", [128, 24], F32)
        lnb_sb = sb("lnb_sb", [128, 24], F32)
        va_sb = sb("va_sb", [128, 8], F32)
        vb_sb = sb("vb_sb", [128, 8], F32)
        vbin_sb = sb("vbin_sb", [128, 8], F32)
        vbout_sb = sb("vbout_sb", [128, 8], F32)
        bprod_sb = sb("bprod_sb", [128, 8], F32)
        A_sb = sb("A_sb", [128, 8], F32)
        a512 = sb("a512", [128, 8], F32)
        a1024 = sb("a1024", [128, 8], F32)
        a1536 = sb("a1536", [128, 8], F32)
        p2048 = sb("p2048", [128, 8], F32)
        p4096 = sb("p4096", [128, 8], F32)
        ptmp = sb("ptmp", [128, 8], F32)
        ptmp2 = sb("ptmp2", [128, 8], F32)
        carry = sb("carry", [128, 8], F32)
        carry2 = sb("carry2", [128, 8], F32)
        coef = sb("coef", [128, 64], F32)
        crep = sb("crep", [128, 64], F32)
        m1_sb = sb("m1_sb", [128, 64], F32)
        mp_sb = sb("mp_sb", [128, 64], F32)
        mp2_sb = sb("mp2_sb", [128, 64], F32)
        gg_sb = sb("gg_sb", [128, 64], F32)
        u_sb = sb("u_sb", [128, 64], F32)
        stA = sb("stA", [128, 32], F32)
        stC = sb("stC", [128, 32], F32)
        no_sb = sb("no_sb", [128, 33], BF16)
        po_sb = sb("po_sb", [128, 33], BF16)
        bco_sb = sb("bco_sb", [1, 128], BF16)
        eps_sb = sb("eps_sb", [1, 1], F32)
        neg1_sb = sb("neg1_sb", [128, 1], F32)
        mu2 = sb("mu2", [1, TQ], F32)
        rstd = sb("rstd", [1, TQ], BF16)
        nmr = sb("nmr", [1, TQ], BF16)
        xbf = sb("xbf", [128, CH * TQ], BF16)
        sqs = sb("sqs", [128, 2 * TQ], BF16)
        y1bf = sb("y1bf", [128, CH * TQ], BF16)
        t1bf = sb("t1bf", [128, TQ], BF16)
        t2bf = sb("t2bf", [128, 4 * TQ], BF16)
        ybf = sb("ybf", [128, CH * TQ], BF16)
        gbf = sb("gbf", [128, 2 * TQ], BF16)
        tpbf = sb("tpbf", [128, TQ], BF16)
        x2st = sb("x2st", [128, 2 * TQ], F32)
        outt = sb("outt", [128, 2 * TQ], F32)

        ps_a = psb("ps_a", [128, 512])
        ps_b = psb("ps_b", [128, 512])
        ps_e = psb("ps_e", [128, 512])
        ps_f = psb("ps_f", [128, 512])
        ps_c = psb("ps_c", [128, 512])
        ps_d = psb("ps_d", [128, 512])
        ps_st = psb("ps_st", [33, 512])
        ps_z = psb("ps_z", [32, 512])
        ps_p4 = [ps_a, ps_b, ps_e, ps_f]

        s_ld = sem("s_ld")
        s_lg = sem("s_lg")
        s_lx = [sem("s_lx0"), sem("s_lx1")]
        s_q = sem("s_q")
        s_wl = [sem("s_wl0"), sem("s_wl1"), sem("s_wl2"), sem("s_wl3")]
        s_lx2 = [sem("s_lx20"), sem("s_lx21")]
        s_wo = [sem("s_wo0"), sem("s_wo1")]
        s_wol = sem("s_wol")
        s_od = [sem("s_od0"), sem("s_od1")]
        s_qc = sem("s_qc")
        s_es2 = [sem("s_es20"), sem("s_es21")]
        s_p2 = sem("s_p2")
        s_mm = sem("s_mm")
        s_pfb = [sem("s_pfb0"), sem("s_pfb1"), sem("s_pfb2"), sem("s_pfb3")]
        s_sq2 = sem("s_sq2")
        s_stp = sem("s_stp")
        s_r1 = sem("s_r1")
        s_r2 = sem("s_r2")
        s_r3 = sem("s_r3")
        s_r4 = sem("s_r4")
        s_r5 = sem("s_r5")
        s_bc = sem("s_bc")
        s_bce = sem("s_bce")
        s_apl = sem("s_apl")
        s_gel = sem("s_gel")
        s_ar = [sem("s_ar0"), sem("s_ar1")]
        s_bpr = sem("s_bpr")
        s_scn = sem("s_scn")
        s_scn2 = sem("s_scn2")
        s_vch = sem("s_vch")
        s_vch2 = sem("s_vch2")
        s_cc = sem("s_cc")
        s_cc2 = sem("s_cc2")
        s_g = sem("s_g")
        s_csc = sem("s_csc")
        s_sil = sem("s_sil")
        s_pou = sem("s_pou")
        s_woc = sem("s_woc")
        s_zm = sem("s_zm")
        s_ze = sem("s_ze")

        def a_bcast(c, n):
            return bass.AP(A_sb.ap().tensor, c, [[8, 128], [0, n]])

        def lbT_slice(w, mc):
            return bass.AP(lbt_sb.ap().tensor, w * D + mc * 128,
                           [[NW * D, 32], [1, 128]])

        def y1t(inst, c):
            if _SECOND.get(inst):
                t = qcbf if c < 4 else esbf
                return t[:, (c % 4) * TQ:(c % 4) * TQ + TQ]
            return y1bf[:, c * TQ:(c + 1) * TQ]

        with nc.Block() as blk:
            @blk.sync
            def _(sync):
                nld = [0]

                def d(out, in_):
                    sync.dma_start(out=out, in_=in_).then_inc(s_ld, 16)
                    nld[0] += 16
                d(bias_sb[:], bass.AP(bias_h, 0, [[8, 128], [1024, NW], [1, 8]]))
                d(lng_sb[:], bass.AP(lng_h, 0, [[8, 128], [1024, 3], [1, 8]]))
                d(lnb_sb[:], bass.AP(lnb_h, 0, [[8, 128], [1024, 3], [1, 8]]))
                d(va_sb[:], va_h.ap())
                d(vb_sb[:], vb_h.ap())
                d(vbin_sb[:], vbin_h.ap())
                d(vbout_sb[:], vbout_h.ap())
                d(m1_sb[:], m1_h.ap())
                d(mp_sb[:], mp_h.ap())
                d(mp2_sb[:], mp2_h.ap())
                d(no_sb[:], no33_c.ap())
                d(po_sb[:], po33_c.ap())
                d(bco_sb[:], bcones.ap())

                def arena_stream(w, mc):
                    cnt = _ar_cnt(w, mc)
                    if cnt >= 2:
                        sync.wait_ge(s_ar[mc % 2], 16 * (cnt - 1))
                    sync.dma_start(
                        out=xnbf[:, mc * 2048 + (w % 2) * 1024:
                                 mc * 2048 + (w % 2) * 1024 + 1024],
                        in_=bass.AP(w_hbm, w * 1048576 + mc * 131072,
                                    [[1024, 128], [1, 1024]]),
                    ).then_inc(s_ar[mc % 2], 16)

                for u in UNITS:
                    if u[0] == "W":
                        w = u[1]
                        for kc in range(CH):
                            i = w * 8 + kc
                            sync.wait_ge(s_p2, 2 * i + 2)
                            sync.dma_start(
                                out=bass.AP(w_hbm, w * 1048576 + kc * 128,
                                            [[1024, 128], [131072, 8], [1, 128]]),
                                in_=wchk[:, (i % 2) * 1024:(i % 2) * 1024 + 1024],
                            ).then_inc(s_wo[i % 2], 16)
                        if w == 3:
                            pos, bank, seq = _GINFO[("p", 1, 0, 1, 7)]
                            sync.wait_ge(s_pfb[bank], seq + 1)
                            sync.wait_ge(s_wo[0], 16 * 4 * 3)
                            sync.wait_ge(s_wo[1], 16 * 4 * 3)
                            for mc in range(CH):
                                arena_stream(2, mc)
                        if w == 5:
                            pos, bank, seq = _GINFO[("p", 3, 0, 1, 7)]
                            sync.wait_ge(s_pfb[bank], seq + 1)
                            sync.wait_ge(s_wo[0], 16 * 4 * 4)
                            sync.wait_ge(s_wo[1], 16 * 4 * 4)
                            for mc in range(CH):
                                arena_stream(3, mc)
                    elif u[0] == "B":
                        _, it, b = u
                        if b == 0:
                            for p in range(2):
                                w = p
                                for mc in range(CH):
                                    j = _stream_j0(it, p, mc)
                                    sync.wait_ge(s_wo[0], 16 * 4 * (w + 1))
                                    sync.wait_ge(s_wo[1], 16 * 4 * (w + 1))
                                    if j >= 4:
                                        sync.wait_ge(
                                            s_mm,
                                            _GINFO[_stream_key0(j - 4)][0] + 1)
                                    sync.dma_start(
                                        out=wstr[:, (j % 4) * 1024:
                                                 (j % 4) * 1024 + 1024],
                                        in_=bass.AP(w_hbm,
                                                    w * 1048576 + mc * 131072,
                                                    [[1024, 128], [1, 1024]]),
                                    ).then_inc(s_wl[j % 4], 16)
                    elif u[0] == "P" and u[1:] == (2, 3, 1):
                        if True:
                            for mc in range(CH):
                                sync.wait_ge(s_wo[0], 16 * 4 * 5)
                                sync.wait_ge(s_wo[1], 16 * 4 * 5)
                                sync.wait_ge(s_mm, _GINFO[("p", 3, 1, 0, mc)][0] + 1)
                                arena_stream(4, mc)
                            for mc in range(CH):
                                sync.wait_ge(s_wo[0], 16 * 4 * 6)
                                sync.wait_ge(s_wo[1], 16 * 4 * 6)
                                sync.wait_ge(s_mm, _GINFO[("p", 3, 1, 1, mc)][0] + 1)
                                arena_stream(5, mc)
                sync.wait_ge(s_od[0], 16 * 16)
                sync.wait_ge(s_od[1], 16 * 16)

            @blk.scalar
            def _(scalar):
                def rows_a(inst):
                    scalar.wait_ge(s_stp, 8 * (inst + 1))
                    scalar.activation(mu2[:], ps_st[0:1, :],
                                      AF.Square).then_inc(s_r2, 1)
                    scalar.wait_ge(s_r3, inst + 1)
                    scalar.activation(mu2[:], nmr[:], AF.Ln, bias=eps_sb[0:1, 0:1])
                    scalar.activation(rstd[:], mu2[:], AF.Exp,
                                      scale=-0.5).then_inc(s_r4, 1)

                def rows_s(inst):
                    rows_a(inst)

                def silus(it):
                    for mc in range(CH):
                        pos2, bank2, seq2 = _GINFO[("p", it, 2, 1, mc)]
                        k = it * 8 + mc
                        scalar.wait_ge(s_pfb[bank2], seq2 + 1)
                        if k >= 2:
                            scalar.wait_ge(s_pou, k - 1)
                        scalar.dma_start(
                            out=x2st[:, (k % 2) * TQ:(k % 2) * TQ + TQ],
                            in_=bass.AP(x_fm, mc * 128 * TOK + it * TQ,
                                        [[TOK, 128], [1, TQ]]),
                        ).then_inc(s_lx2[k % 2], 16)
                        if k >= 2:
                            kp = k - 2
                            scalar.dma_start(
                                out=bass.AP(out_fm,
                                            (kp % 8) * 128 * TOK + (kp // 8) * TQ,
                                            [[TOK, 128], [1, TQ]]),
                                in_=outt[:, (kp % 2) * TQ:(kp % 2) * TQ + TQ],
                            ).then_inc(s_od[kp % 2], 16)
                        scalar.activation(gbf[:, (mc % 2) * TQ:(mc % 2) * TQ + TQ],
                                          Sst[:, it * 4096 + mc * TQ:
                                              it * 4096 + mc * TQ + TQ],
                                          AF.Silu).then_inc(s_sil, 1)

                scalar.wait_ge(s_ld, 13 * 16)
                scalar.activation(A_sb[:], va_sb[:], AF.Sigmoid).then_inc(s_wol, 1)
                bufb_open = [False]
                for u in UNITS:
                    if u[0] == "W":
                        continue
                    elif u[0] == "A":
                        tq = u[1]
                        inst = _inst_a(tq)
                        rows_s(inst)
                        if tq == 0:
                            scalar.wait_ge(s_bpr, 1)
                        for mc in range(CH):
                            pos, bank, seq = _GINFO[("h", tq, mc)]
                            scalar.wait_ge(s_mm, pos + 1)
                            scalar.activation(
                                bhbf[:, mc * TOK + tq * TQ: mc * TOK + tq * TQ + TQ],
                                ps_p4[bank][:], AF.Identity,
                                bias=bprod_sb[:, mc:mc + 1],
                                scale=vb_sb[:, mc:mc + 1],
                            ).then_inc(s_pfb[bank], 1)
                    elif u[0] == "B" or u[0] == "P":
                        def zev(it, b, p):
                            zi = _ZINFO[(it, b, p)]
                            scalar.wait_ge(s_zm, zi + 1)
                            if zi >= 1:
                                pk = _ZREV[zi - 1]
                                scalar.wait_ge(
                                    s_mm,
                                    _GINFO[("p", pk[0], pk[1], pk[2], 7)][0] + 1)
                            scalar.activation(zbf[:], ps_z[:],
                                              AF.Copy).then_inc(s_ze, 1)

                        def evacs_rows(it, b, inst):
                            if _SECOND.get(inst) and not bufb_open[0]:
                                scalar.wait_ge(s_p2, 96)
                                bufb_open[0] = True
                            for mc in range(CH):
                                pos, bank, seq = _GINFO[("p", it, b, 0, mc)]
                                scalar.wait_ge(s_mm, pos + 1)
                                scalar.activation(
                                    y1t(inst, mc),
                                    ps_p4[bank][:], AF.Identity,
                                    bias=bias_sb[:, 2 * b * 8 + mc:
                                                 2 * b * 8 + mc + 1],
                                ).then_inc(s_pfb[bank], 1)
                            rows_a(inst)

                        def gelus(it, b, inst):
                            for mc in range(CH):
                                k = inst * 8 + mc
                                scalar.wait_ge(s_apl, k + 1)
                                scalar.wait_ge(s_stp, k + 1)
                                scalar.activation(
                                    y1t(inst, mc),
                                    t2bf[:, (mc % 4) * TQ:(mc % 4) * TQ + TQ],
                                    AF.Gelu,
                                    scale=lng_sb[:, b * 8 + mc:b * 8 + mc + 1],
                                    bias=lnb_sb[:, b * 8 + mc:b * 8 + mc + 1],
                                ).then_inc(s_gel, 1)

                        if u[0] == "B":
                            _, it, b = u
                            inst = _inst_b(it, b)
                            zev(it, b, 0)
                            evacs_rows(it, b, inst)
                            gelus(it, b, inst)
                            zev(it, b, 1)
                            if b == 2:
                                silus(it)
                        else:
                            _, itX, itY, b = u
                            iX, iY = _inst_b(itX, b), _inst_b(itY, b)
                            zev(itX, b, 0)
                            evacs_rows(itX, b, iX)
                            zev(itY, b, 0)
                            gelus(itX, b, iX)
                            evacs_rows(itY, b, iY)
                            zev(itX, b, 1)
                            gelus(itY, b, iY)
                            zev(itY, b, 1)
                            if b == 2:
                                silus(itX)
                                silus(itY)
                                if itY == 3:
                                    for kp in (30, 31):
                                        scalar.wait_ge(s_pou, kp + 1)
                                        scalar.dma_start(
                                            out=bass.AP(out_fm,
                                                        (kp % 8) * 128 * TOK
                                                        + (kp // 8) * TQ,
                                                        [[TOK, 128], [1, TQ]]),
                                            in_=outt[:, (kp % 2) * TQ:
                                                     (kp % 2) * TQ + TQ],
                                        ).then_inc(s_od[kp % 2], 16)

            @blk.gpsimd
            def _(gpsimd):
                gp_units = [("A", 0), ("A", 1), ("W", 0), ("W", 1),
                            ("AL", 2), ("AL", 3), ("W", 2), ("W", 3),
                            ("W", 4), ("W", 5), ("AX", 3)]
                qsem = [s_q, s_qc]

                def qload(w, kc):
                    g = w * 8 + kc
                    gpsimd.dma_start(
                        out=qbf[:, (kc % 2) * 1024:(kc % 2) * 1024 + 1024],
                        in_=bass.AP(q_t, w * D * D + kc * 128 * D,
                                    [[D, 128], [1, D]]),
                    ).then_inc(qsem[g % 2], 16)

                def esload(w, kc):
                    g = w * 8 + kc
                    gpsimd.dma_start(
                        out=esbf[:, (g % 2) * 1024:(g % 2) * 1024 + 1024],
                        in_=bass.AP(sc_t, w * 64 * D + kc * 8 * D,
                                    [[D, 8], [0, 16], [1, D]]),
                    ).then_inc(s_es2[g % 2], 16)

                for u in gp_units:
                    if u[0] == "W":
                        w = u[1]
                        if w == 0:
                            gpsimd.dma_start(
                                out=wio[:],
                                in_=bass.AP(wint_h, 0,
                                            [[D, 128], [128 * D, CH], [1, D]]),
                            ).then_inc(s_woc, 16)
                        qload(w, 0)
                        qload(w, 1)
                        esload(w, 0)
                        for kc in range(CH):
                            i = w * 8 + kc
                            hfq = (i % 2) * 1024
                            gpsimd.wait_ge(qsem[i % 2], 16 * (i // 2 + 1))
                            gpsimd.tensor_scalar(
                                qcbf[:, hfq:hfq + 1024],
                                qbf[:, hfq:hfq + 1024],
                                1.0 / 7.5, -1.0, OP.mult, OP.add)
                            if kc + 2 < CH:
                                qload(w, kc + 2)
                            if kc + 1 < CH:
                                esload(w, kc + 1)
                            gpsimd.wait_ge(s_es2[i % 2], 16 * (i // 2 + 1))
                            if i >= 2:
                                gpsimd.wait_ge(s_wo[i % 2], 16 * (i // 2))
                            gpsimd.tensor_mul(
                                wchk[:, hfq:hfq + 1024],
                                qcbf[:, hfq:hfq + 1024],
                                esbf[:, hfq:hfq + 1024],
                            ).then_inc(s_p2, 2)
                    elif u[0] in ("A", "AL"):
                        tq = u[1]
                        xdst = xbf if tq % 2 == 0 else ybf
                        for p in range(4):
                            kp = tq * 4 + p
                            if tq >= 2:
                                gpsimd.wait_ge(s_apl,
                                               8 * _inst_a(tq - 2) + 2 * (p + 1))
                            if kp >= 2:
                                gpsimd.wait_ge(s_lx[kp % 2], 16 * (kp // 2))
                            gpsimd.dma_start(
                                out=xdst[:, 2 * p * TQ:2 * p * TQ + 2 * TQ],
                                in_=bass.AP(x_fm, 2 * p * 128 * TOK + tq * TQ,
                                            [[TOK, 128], [128 * TOK, 2], [1, TQ]]),
                            ).then_inc(s_lx[kp % 2], 16)
                        if tq == 1:
                            for w in range(NW):
                                gpsimd.dma_start(
                                    out=latall[:, w * 256:(w + 1) * 256],
                                    in_=bass.AP(lat_h, w * D * 32,
                                                [[32, 128], [128 * 32, 8],
                                                 [1, 32]]),
                                ).then_inc(s_lg, 16)
                            gpsimd.dma_start(out=lbt_sb[:],
                                             in_=lbt_h.ap()).then_inc(s_lg, 16)
                    elif u[0] == "AX":
                        if True:
                            gpsimd.wait_ge(s_mm, _GINFO[("h", 3, 7)][0] + 1)
                            gpsimd.dma_start(
                                out=wio[:],
                                in_=bass.AP(woutt_h, 0,
                                            [[D, 128], [128 * D, CH], [1, D]]),
                            ).then_inc(s_woc, 16)
                            gpsimd.wait_ge(s_scn2, 32)
                            gpsimd.dma_start(out=f_bnc.ap(),
                                             in_=stA[:, 24:32]).then_inc(s_cc, 16)
                            gpsimd.wait_ge(s_cc, 16)
                            gpsimd.collective_compute(
                                "AllGather", OP.bypass,
                                replica_groups=[list(range(NCORE))],
                                ins=[f_bnc.ap().opt()],
                                outs=[g_bnc.ap().opt()],
                            ).then_inc(s_cc2, 1)
                            gpsimd.wait_ge(s_cc2, 1)
                            gpsimd.dma_start(
                                out=gg_sb[:],
                                in_=bass.AP(g_bnc, 0, [[8, 128], [1024, 8], [1, 8]]),
                            ).then_inc(s_g, 16)

            @blk.tensor
            def _(tensor):
                tensor.wait_ge(s_ld, 13 * 16)

                def proj_group(key, lhsT_tile, rhs_tiles, lora_w=None):
                    pos, bank, seq = _GINFO[key]
                    if seq >= 1:
                        tensor.wait_ge(s_pfb[bank], seq)
                    last = 7 if lora_w is None else 8
                    for kc in range(CH):
                        mm = tensor.matmul(
                            ps_p4[bank][:], lhsT_tile(kc), rhs_tiles(kc),
                            start=(kc == 0), stop=(kc == last))
                    if lora_w is not None:
                        w, mc = lora_w
                        mm = tensor.matmul(ps_p4[bank][:], lbT_slice(w, mc), zbf[:],
                                           start=False, stop=True)
                    mm.then_inc(s_mm, 1)

                def zmms(it, b, p, rhs_tiles):
                    zi = _ZINFO[(it, b, p)]
                    w = 2 * b + p
                    if zi >= 1:
                        tensor.wait_ge(s_ze, zi)
                    for kc in range(CH):
                        mm = tensor.matmul(
                            ps_z[:],
                            latall[:, w * 256 + kc * 32: w * 256 + kc * 32 + 32],
                            rhs_tiles(kc),
                            start=(kc == 0), stop=(kc == 7))
                    mm.then_inc(s_zm, 1)

                def stats_pair(inst, c, src_bf):
                    k = inst * 8 + c
                    tensor.wait_ge(s_sq2, k + 1)
                    tensor.matmul(ps_st[:], no_sb[:], src_bf,
                                  start=(c == 0), stop=False)
                    mm = tensor.matmul(ps_st[:], po_sb[:],
                                       sqs[:, (c % 2) * TQ:(c % 2) * TQ + TQ],
                                       start=False, stop=(c == 7))
                    mm.then_inc(s_stp, 1)

                def bcast_inst(inst):
                    tensor.wait_ge(s_r4, inst + 1)
                    tensor.wait_ge(s_r5, inst + 1)
                    if inst >= 1:
                        tensor.wait_ge(s_apl, 8 * inst)
                    tensor.matmul(ps_c[:], bco_sb[:], rstd[:], start=True, stop=True)
                    tensor.matmul(ps_d[:], bco_sb[:], nmr[:],
                                  start=True, stop=True).then_inc(s_bc, 1)

                def sproj(it):
                    tensor.wait_ge(s_woc, 32)
                    tensor.wait_ge(s_csc, 8 * (it + 1))
                    for mc in range(CH):
                        proj_group(
                            ("s", it, mc),
                            lambda kc, mc=mc: wio[:, kc * D + mc * 128:
                                                  kc * D + mc * 128 + 128],
                            lambda kc: ybf[:, kc * TQ:(kc + 1) * TQ])

                def wtile(it, b, p, mc):
                    w = 2 * b + p
                    if b == 0:
                        j = _stream_j0(it, p, mc)
                        tensor.wait_ge(s_wl[j % 4], 16 * (j // 4 + 1))
                        return lambda kc, j=j: wstr[
                            :, (j % 4) * 1024 + kc * 128:
                            (j % 4) * 1024 + kc * 128 + 128]
                    tensor.wait_ge(s_ar[mc % 2], 16 * (_ar_cnt(w, mc) + 1))
                    base = mc * 2048 + (w % 2) * 1024
                    return lambda kc, base=base: xnbf[
                        :, base + kc * 128: base + kc * 128 + 128]

                def t_first(it, b, inst, do_bcast=True):
                    if b == 0:
                        rhs = lambda kc: xnbf[:, kc * TOK + it * TQ:
                                              kc * TOK + it * TQ + TQ]
                    else:
                        need = {}
                        for mc in range(CH):
                            pos, bank, seq = _GINFO[("p", it, b - 1, 1, mc)]
                            need[bank] = max(need.get(bank, -1), seq)
                        for bank in sorted(need):
                            tensor.wait_ge(s_pfb[bank], need[bank] + 1)
                        rhs = lambda kc: Sst[:, it * 4096 + kc * TQ:
                                             it * 4096 + kc * TQ + TQ]
                    tensor.wait_ge(s_r5, inst)
                    if (it, b) == (0, 0):
                        tensor.wait_ge(s_lg, 112)
                    zmms(it, b, 0, rhs)
                    zi1 = _ZINFO[(it, b, 0)]
                    for mc in range(CH):
                        lt = wtile(it, b, 0, mc)
                        if mc == 0:
                            tensor.wait_ge(s_ze, zi1 + 1)
                        proj_group(("p", it, b, 0, mc), lt, rhs,
                                   lora_w=(2 * b, mc))
                        if mc >= 2:
                            stats_pair(inst, mc - 2, y1t(inst, mc - 2))
                    for c in (6, 7):
                        stats_pair(inst, c, y1t(inst, c))
                    if do_bcast:
                        bcast_inst(inst)

                def t_second(it, b, inst):
                    tensor.wait_ge(s_gel, 8 * (_GPRE[inst] + 1))
                    zmms(it, b, 1, lambda kc: y1t(inst, kc))
                    zi2 = _ZINFO[(it, b, 1)]
                    for mc in range(CH):
                        lt = wtile(it, b, 1, mc)
                        if mc == 0:
                            tensor.wait_ge(s_ze, zi2 + 1)
                        proj_group(("p", it, b, 1, mc), lt,
                                   lambda kc: y1t(inst, kc),
                                   lora_w=(2 * b + 1, mc))

                for u in UNITS:
                    if u[0] == "W":
                        continue
                    if u[0] == "A":
                        tq = u[1]
                        xsrc = xbf if tq % 2 == 0 else ybf
                        inst = _inst_a(tq)
                        tensor.wait_ge(s_r5, inst)
                        for c in range(CH):
                            stats_pair(inst, c, xsrc[:, c * TQ:(c + 1) * TQ])
                        bcast_inst(inst)
                        tensor.wait_ge(s_apl, 8 * (inst + 1))
                        if tq == 0:
                            tensor.wait_ge(s_woc, 16)
                        for mc in range(CH):
                            proj_group(
                                ("h", tq, mc),
                                lambda kc, mc=mc: wio[:, kc * D + mc * 128:
                                                      kc * D + mc * 128 + 128],
                                lambda kc: xnbf[:, kc * TOK + tq * TQ:
                                                kc * TOK + tq * TQ + TQ])
                    elif u[0] == "B":
                        _, it, b = u
                        inst = _inst_b(it, b)
                        t_first(it, b, inst)
                        t_second(it, b, inst)
                    else:
                        _, itX, itY, b = u
                        iX, iY = _inst_b(itX, b), _inst_b(itY, b)
                        t_first(itX, b, iX)
                        t_first(itY, b, iY)
                        t_second(itX, b, iX)
                        t_second(itY, b, iY)
                        if b == 2:
                            sproj(itX)
                            sproj(itY)

            @blk.vector
            def _(vector):
                vc = [0]

                def g(instr):
                    instr.then_inc(s_vch, 1)
                    vc[0] += 1
                    vector.wait_ge(s_vch, vc[0])

                g(vector.memset(eps_sb[:], EPS))
                g(vector.memset(neg1_sb[:], -1.0))
                vector.wait_ge(s_wol, 1)
                bufs = [ptmp, ptmp2]
                g(vector.tensor_mul(ptmp[:], A_sb[:], A_sb[:]))
                cur = ptmp
                for i in range(8):
                    dst = a512 if i == 7 else bufs[(i + 1) % 2]
                    g(vector.tensor_mul(dst[:], cur[:], cur[:]))
                    cur = dst
                g(vector.tensor_mul(a1024[:], a512[:], a512[:]))
                g(vector.tensor_mul(a1536[:], a1024[:], a512[:]))
                g(vector.tensor_mul(p2048[:], a1024[:], a1024[:]))
                g(vector.tensor_mul(p4096[:], p2048[:], p2048[:]))
                vector.tensor_mul(bprod_sb[:], vb_sb[:],
                                  vbin_sb[:]).then_inc(s_bpr, 1)
                g(vector.tensor_copy(
                    crep[:], bass.AP(p2048.ap().tensor, 0, [[8, 128], [0, 8], [1, 8]])))
                g(vector.tensor_mul(u_sb[:], mp_sb[:], crep[:]))
                g(vector.tensor_add(gg_sb[:], m1_sb[:], u_sb[:]))
                g(vector.tensor_copy(
                    crep[:], bass.AP(p4096.ap().tensor, 0, [[8, 128], [0, 8], [1, 8]])))
                g(vector.tensor_mul(u_sb[:], mp2_sb[:], crep[:]))
                g(vector.tensor_add(coef[:], gg_sb[:], u_sb[:]))

                def rows_inst(inst):
                    vector.wait_ge(s_r2, inst + 1)
                    vector.tensor_sub(nmr[:], ps_st[32:33, :],
                                      mu2[:]).then_inc(s_r3, 1)
                    vector.wait_ge(s_r4, inst + 1)
                    vector.tensor_mul(nmr[:], ps_st[0:1, :],
                                      rstd[:]).then_inc(s_r5, 1)

                def apply_inst(inst, src_bf, dst, gel_guard=False):
                    vector.wait_ge(s_bc, inst + 1)
                    for c in range(CH):
                        if gel_guard:
                            g = 8 * _GPRE[inst] + c - 3
                            if g > 0:
                                vector.wait_ge(s_gel, g)
                        vector.tensor_mul(t1bf[:], src_bf(c), ps_c[:])
                        vector.tensor_add(dst(c), t1bf[:], ps_d[:]).then_inc(s_apl, 1)

                def wait_evac(key):
                    pos, bank, seq = _GINFO[key]
                    vector.wait_ge(s_pfb[bank], seq + 1)

                def pstt_out(it):
                    for mc in range(CH):
                        pos, bank, seq = _GINFO[("s", it, mc)]
                        k = it * 8 + mc
                        vector.wait_ge(s_mm, pos + 1)
                        vector.wait_ge(s_sil, k + 1)
                        vector.scalar_tensor_tensor(
                            tpbf[:], ps_p4[bank][:],
                            vbout_sb[:, mc:mc + 1],
                            gbf[:, (mc % 2) * TQ:(mc % 2) * TQ + TQ],
                            OP.add, OP.mult,
                        ).then_inc(s_pfb[bank], 1)
                        vector.wait_ge(s_lx2[k % 2], 16 * (k // 2 + 1))
                        if k >= 2:
                            vector.wait_ge(s_od[k % 2], 16 * (k // 2))
                        vector.tensor_add(outt[:, (k % 2) * TQ:(k % 2) * TQ + TQ],
                                          x2st[:, (k % 2) * TQ:(k % 2) * TQ + TQ],
                                          tpbf[:]).then_inc(s_pou, 1)

                def cscans(it):
                    if it > 0:
                        vector.wait_ge(s_mm, _GINFO[("s", it - 1, 7)][0] + 1)
                    for c in range(CH):
                        init = (carry[:, c:c + 1] if it == 0
                                else stC[:, it * 8 + c: it * 8 + c + 1])
                        vector.tensor_tensor_scan(
                            ybf[:, c * TQ:(c + 1) * TQ], a_bcast(c, TQ),
                            bhbf[:, c * TOK + it * TQ: c * TOK + it * TQ + TQ],
                            init, OP.mult, OP.add).then_inc(s_csc, 1)

                for u in UNITS:
                    if u[0] == "W":
                        continue
                    elif u[0] == "A":
                        tq = u[1]
                        xsrc = xbf if tq % 2 == 0 else ybf
                        inst = _inst_a(tq)
                        for c in range(CH):
                            k = inst * 8 + c
                            kp = tq * 4 + c // 2
                            vector.wait_ge(s_lx[kp % 2], 16 * (kp // 2 + 1))
                            if k >= 2:
                                vector.wait_ge(s_stp, k - 1)
                            vector.tensor_mul(sqs[:, (c % 2) * TQ:(c % 2) * TQ + TQ],
                                              xsrc[:, c * TQ:(c + 1) * TQ],
                                              xsrc[:, c * TQ:(c + 1) * TQ]).then_inc(s_sq2, 1)
                        rows_inst(inst)
                        apply_inst(
                            inst,
                            lambda c: xsrc[:, c * TQ:(c + 1) * TQ],
                            lambda c: xnbf[:, c * TOK + tq * TQ: c * TOK + tq * TQ + TQ])
                        for c in range(CH):
                            wait_evac(("h", tq, c))
                            init = 0.0 if tq == 0 else stA[:, (tq - 1) * 8 + c:
                                                           (tq - 1) * 8 + c + 1]
                            vector.tensor_tensor_scan(
                                tpbf[:], a_bcast(c, TQ),
                                bhbf[:, c * TOK + tq * TQ: c * TOK + tq * TQ + TQ],
                                init, OP.mult, OP.add).then_inc(s_scn, 1)
                            vector.wait_ge(s_scn, tq * 8 + c + 1)
                            vector.tensor_copy(stA[:, tq * 8 + c: tq * 8 + c + 1],
                                               tpbf[:, TQ - 1:TQ]).then_inc(s_scn2, 1)
                    elif u[0] == "B" or u[0] == "P":
                        def v_first(it, b, inst):
                            for mc in range(CH):
                                k = inst * 8 + mc
                                wait_evac(("p", it, b, 0, mc))
                                if k >= 2:
                                    vector.wait_ge(s_stp, k - 1)
                                vector.tensor_mul(
                                    sqs[:, (mc % 2) * TQ:(mc % 2) * TQ + TQ],
                                    y1t(inst, mc),
                                    y1t(inst, mc)).then_inc(s_sq2, 1)
                            rows_inst(inst)
                            apply_inst(inst,
                                       lambda c: y1t(inst, c),
                                       lambda c: t2bf[:, (c % 4) * TQ:
                                                      (c % 4) * TQ + TQ],
                                       gel_guard=True)
                        def v_second(it, b, inst):
                            for mc in range(CH):
                                pos, bank, seq = _GINFO[("p", it, b, 1, mc)]
                                vector.wait_ge(s_mm, pos + 1)
                                sdst = Sst[:, it * 4096 + mc * TQ:
                                           it * 4096 + mc * TQ + TQ]
                                if b == 0:
                                    res_in = xnbf[:, mc * TOK + it * TQ:
                                                  mc * TOK + it * TQ + TQ]
                                else:
                                    res_in = sdst
                                vector.scalar_tensor_tensor(
                                    sdst, ps_p4[bank][:],
                                    bias_sb[:, (2 * b + 1) * 8 + mc:
                                            (2 * b + 1) * 8 + mc + 1],
                                    res_in, OP.add, OP.add,
                                ).then_inc(s_pfb[bank], 1)

                        if u[0] == "B":
                            _, it, b = u
                            inst = _inst_b(it, b)
                            v_first(it, b, inst)
                            v_second(it, b, inst)
                        else:
                            _, itX, itY, b = u
                            iX, iY = _inst_b(itX, b), _inst_b(itY, b)
                            v_first(itX, b, iX)
                            v_first(itY, b, iY)
                            v_second(itX, b, iX)
                            v_second(itY, b, iY)
                            if b == 2:
                                pstt_out(itX)
                                if itX + 1 < 4:
                                    cscans(itX + 1)
                                pstt_out(itY)
                                if itY + 1 < 4:
                                    cscans(itY + 1)
                        if u[0] == "P" and u[1:] == (0, 1, 1):
                            vector.wait_ge(s_g, 16)
                            vcc = [0]

                            def gc(instr):
                                instr.then_inc(s_vch2, 1)
                                vcc[0] += 1
                                vector.wait_ge(s_vch2, vcc[0])

                            gc(vector.tensor_mul(u_sb[:], gg_sb[:], coef[:]))
                            cbufs = [carry, carry2]
                            gc(vector.tensor_add(carry[:], u_sb[:, 0:8],
                                                 u_sb[:, 8:16]))
                            cur = 0
                            for j in range(2, 8):
                                gc(vector.tensor_add(cbufs[(cur + 1) % 2][:],
                                                     cbufs[cur][:],
                                                     u_sb[:, j * 8:(j + 1) * 8]))
                                cur = (cur + 1) % 2
                            if cur != 0:
                                gc(vector.tensor_copy(carry[:], carry2[:]))
                            for tq2, ap in ((1, a512), (2, a1024), (3, a1536)):
                                gc(vector.tensor_mul(ptmp[:], ap[:], carry[:]))
                                gc(vector.tensor_add(stC[:, tq2 * 8:(tq2 + 1) * 8],
                                                     stA[:, (tq2 - 1) * 8:tq2 * 8],
                                                     ptmp[:]))
                            cscans(0)

    return nc


_NC_CACHE = {}


def _get_nc():
    if "nc" not in _NC_CACHE:
        _NC_CACHE["nc"] = build_nc()
    return _NC_CACHE["nc"]


def _vec128(v):
    return np.ascontiguousarray(np.asarray(v, dtype=np.float32).reshape(8, 128).T)


def kernel(L0_attn_x, L0_attn_w_in, L0_attn_b_in, L0_attn_a, L0_attn_b,
           L0_attn_w_out, L0_attn_b_out, L0_ffn_qweights, L0_ffn_scales,
           L0_ffn_biases, L0_ffn_lora_a, L0_ffn_lora_b, L0_ffn_ln_gamma,
           L0_ffn_ln_beta):
    x = np.asarray(L0_attn_x, dtype=np.float32).reshape(16384, D)
    q_t = np.ascontiguousarray(np.asarray(L0_ffn_qweights, dtype=np.int32)
                               .transpose(0, 2, 1))
    sc_t = np.ascontiguousarray(np.asarray(L0_ffn_scales, dtype=np.float32)
                                .transpose(0, 2, 1))
    la = np.asarray(L0_ffn_lora_a, dtype=np.float32)
    lat = np.ascontiguousarray(la.transpose(0, 2, 1))
    lbt = np.asarray(L0_ffn_lora_b, dtype=np.float32).transpose(0, 2, 1)
    lbt_all = np.ascontiguousarray(lbt.transpose(1, 0, 2).reshape(32, NW * D))
    biases = np.ascontiguousarray(
        np.asarray(L0_ffn_biases, dtype=np.float32).reshape(NW, 8, 128)
        .transpose(0, 2, 1))
    lng = np.ascontiguousarray(
        np.asarray(L0_ffn_ln_gamma, dtype=np.float32).reshape(3, 8, 128)
        .transpose(0, 2, 1))
    lnb = np.ascontiguousarray(
        np.asarray(L0_ffn_ln_beta, dtype=np.float32).reshape(3, 8, 128)
        .transpose(0, 2, 1))
    va = _vec128(L0_attn_a)
    vb = _vec128(L0_attn_b)
    vbin = _vec128(L0_attn_b_in)
    vbout = _vec128(L0_attn_b_out)
    wint = np.ascontiguousarray(np.asarray(L0_attn_w_in, dtype=np.float32).T)
    woutt = np.ascontiguousarray(np.asarray(L0_attn_w_out, dtype=np.float32).T)

    in_maps = []
    for c in range(NCORE):
        k = c % 4
        m1 = np.zeros((128, 64), np.float32)
        mp = np.zeros((128, 64), np.float32)
        mp2 = np.zeros((128, 64), np.float32)
        base = (c // 4) * 4
        if k >= 1:
            m1[:, (base + k - 1) * 8:(base + k) * 8] = 1.0
        if k >= 2:
            mp[:, (base + k - 2) * 8:(base + k - 1) * 8] = 1.0
        if k >= 3:
            mp2[:, (base + k - 3) * 8:(base + k - 2) * 8] = 1.0
        x_fmc = np.ascontiguousarray(x[c * TOK:(c + 1) * TOK, :].T)
        in_maps.append(dict(
            x_fm=x_fmc, q_t=q_t, sc_t=sc_t, lat_h=lat, lbt_h=lbt_all,
            bias_h=biases, lng_h=lng, lnb_h=lnb,
            va_h=va, vb_h=vb, vbin_h=vbin, vbout_h=vbout,
            wint_h=wint, woutt_h=woutt,
            m1_h=m1, mp_h=mp, mp2_h=mp2,
        ))

    nc = _get_nc()
    res = run_bass_kernel_spmd(nc, in_maps, core_ids=list(range(NCORE)))
    out = np.empty((16384, D), dtype=np.float32)
    for c in range(NCORE):
        out[c * TOK:(c + 1) * TOK, :] = res.results[c]["out_fm"].T
    return out.reshape(2, 8192, D)


# version 25
# speedup vs baseline: 1.1952x; 1.0843x over previous
import contextlib
import numpy as np
import ml_dtypes
import concourse.bass as bass
import concourse.mybir as mybir
from concourse.bass_utils import run_bass_kernel_spmd

F32 = mybir.dt.float32
BF16 = mybir.dt.bfloat16
I32 = mybir.dt.int32
AF = mybir.ActivationFunctionType
OP = mybir.AluOpType

NCORE = 8
TOK = 2048
D = 1024
CH = 8
TQ = 512
NTQ = 4
NW = 6
EPS = 1e-5
BF = ml_dtypes.bfloat16

UNITS = [("W", 0), ("W", 1), ("A", 0), ("A", 1), ("B", 0, 0), ("W", 2),
         ("A", 2), ("B", 1, 0), ("W", 3), ("A", 3), ("B", 2, 0), ("W", 4),
         ("B", 3, 0), ("W", 5),
         ("P", 0, 1, 1), ("P", 2, 3, 1), ("P", 0, 1, 2), ("P", 2, 3, 2)]


def _build_tables():
    ginfo = {}
    inst_of = {}
    gelu_pre = {}
    zinfo = {}
    second = {}
    order = []
    ninst = 0
    nb = 0
    nz = 0
    for u in UNITS:
        if u[0] == "W":
            continue
        if u[0] == "A":
            tq = u[1]
            inst_of[u] = ninst
            gelu_pre[ninst] = nb
            ninst += 1
            for mc in range(8):
                order.append(("h", tq, mc))
        elif u[0] == "B":
            _, it, b = u
            inst_of[u] = ninst
            gelu_pre[ninst] = nb
            ninst += 1
            nb += 1
            zinfo[(it, b, 0)] = nz
            nz += 1
            for mc in range(8):
                order.append(("p", it, b, 0, mc))
            zinfo[(it, b, 1)] = nz
            nz += 1
            for mc in range(8):
                order.append(("p", it, b, 1, mc))
        else:
            _, itX, itY, b = u
            for idx, it in enumerate((itX, itY)):
                inst_of[("B", it, b)] = ninst
                gelu_pre[ninst] = nb
                if idx == 1:
                    second[ninst] = True
                ninst += 1
                nb += 1
            for it in (itX, itY):
                zinfo[(it, b, 0)] = nz
                nz += 1
            for it in (itX, itY):
                for mc in range(8):
                    order.append(("p", it, b, 0, mc))
            for it in (itX, itY):
                zinfo[(it, b, 1)] = nz
                nz += 1
            for it in (itX, itY):
                for mc in range(8):
                    order.append(("p", it, b, 1, mc))
            if b == 2:
                for it in (itX, itY):
                    for mc in range(8):
                        order.append(("s", it, mc))
    ginfo_banks = [0, 0, 0, 0]
    for pos, key in enumerate(order):
        bank = pos % 4
        ginfo[key] = (pos, bank, ginfo_banks[bank])
        ginfo_banks[bank] += 1
    return ginfo, inst_of, gelu_pre, zinfo, second


_GINFO, _INST, _GPRE, _ZINFO, _SECOND = _build_tables()
_ZREV = {v: k for k, v in _ZINFO.items()}


def _inst_a(tq):
    return _INST[("A", tq)]


def _inst_b(it, b):
    return _INST[("B", it, b)]


def _stream_j0(it, p, mc):
    return it * 16 + p * 8 + mc


def _stream_key0(j):
    return ("p", j // 16, 0, (j % 16) // 8, j % 8)


def _ar_cnt(w, mc):
    return (w - 2) * 4 + mc // 2


def build_nc():
    nc = bass.Bass("TRN2", target_bir_lowering=False, detect_race_conditions=False)

    x_fm = nc.dram_tensor("x_fm", [D, TOK], F32, kind="ExternalInput")
    q_t = nc.dram_tensor("q_t", [NW, D, D], I32, kind="ExternalInput")
    sc_t = nc.dram_tensor("sc_t", [NW, 64, D], F32, kind="ExternalInput")
    lat_h = nc.dram_tensor("lat_h", [NW, D, 32], F32, kind="ExternalInput")
    lbt_h = nc.dram_tensor("lbt_h", [32, NW * D], F32, kind="ExternalInput")
    bias_h = nc.dram_tensor("bias_h", [NW, 128, 8], F32, kind="ExternalInput")
    lng_h = nc.dram_tensor("lng_h", [3, 128, 8], F32, kind="ExternalInput")
    lnb_h = nc.dram_tensor("lnb_h", [3, 128, 8], F32, kind="ExternalInput")
    va_h = nc.dram_tensor("va_h", [128, 8], F32, kind="ExternalInput")
    vb_h = nc.dram_tensor("vb_h", [128, 8], F32, kind="ExternalInput")
    vbin_h = nc.dram_tensor("vbin_h", [128, 8], F32, kind="ExternalInput")
    vbout_h = nc.dram_tensor("vbout_h", [128, 8], F32, kind="ExternalInput")
    wint_h = nc.dram_tensor("wint_h", [D, D], F32, kind="ExternalInput")
    woutt_h = nc.dram_tensor("woutt_h", [D, D], F32, kind="ExternalInput")
    m1_h = nc.dram_tensor("m1_h", [128, 64], F32, kind="ExternalInput")
    mp_h = nc.dram_tensor("mp_h", [128, 64], F32, kind="ExternalInput")
    mp2_h = nc.dram_tensor("mp2_h", [128, 64], F32, kind="ExternalInput")
    out_fm = nc.dram_tensor("out_fm", [D, TOK], F32, kind="ExternalOutput")

    w_hbm = nc.dram_tensor("w_hbm", [NW * 8 * 128 * 8 * 128], BF16)
    f_bnc = nc.dram_tensor("f_bnc", [128, 8], F32)
    g_bnc = nc.dram_tensor("g_bnc", [128 * NCORE, 8], F32)

    no33 = np.zeros((128, 33), dtype=BF)
    no33[:, 0] = -1.0 / D
    po33 = np.zeros((128, 33), dtype=BF)
    po33[:, 32] = 1.0 / D
    no33_c = nc.inline_tensor(no33, "no33c")
    po33_c = nc.inline_tensor(po33, "po33c")
    bcones = nc.inline_tensor(np.ones((1, 128), dtype=BF), "bconesc")

    es = contextlib.ExitStack()
    with es:
        sb = lambda name, shape, dt: es.enter_context(nc.sbuf_tensor(name, shape, dt))
        psb = lambda name, shape: es.enter_context(nc.psum_tensor(name, shape, F32))
        sem = lambda name: es.enter_context(nc.semaphore(name))

        xnbf = sb("xnbf", [128, CH * TOK], BF16)
        bhbf = sb("bhbf", [128, CH * TOK], BF16)
        Sst = sb("Sst", [128, NTQ * CH * TQ], BF16)
        wio = sb("wio", [128, CH * D], BF16)
        wstr = sb("wstr", [128, 4 * 1024], BF16)
        qbf = sb("qbf", [128, 2 * 1024], BF16)
        qcbf = sb("qcbf", [128, 2 * 1024], BF16)
        esbf = sb("esbf", [128, 2 * 1024], BF16)
        wchk = sb("wchk", [128, 2 * 1024], BF16)
        latall = sb("latall", [128, NW * 256], BF16)
        lbt_sb = sb("lbt_sb", [32, NW * D], BF16)
        zbf = sb("zbf", [32, TQ], BF16)
        bias_sb = sb("bias_sb", [128, NW * 8], F32)
        lng_sb = sb("lng_sb", [128, 24], F32)
        lnb_sb = sb("lnb_sb", [128, 24], F32)
        va_sb = sb("va_sb", [128, 8], F32)
        vb_sb = sb("vb_sb", [128, 8], F32)
        vbin_sb = sb("vbin_sb", [128, 8], F32)
        vbout_sb = sb("vbout_sb", [128, 8], F32)
        bprod_sb = sb("bprod_sb", [128, 8], F32)
        A_sb = sb("A_sb", [128, 8], F32)
        a512 = sb("a512", [128, 8], F32)
        a1024 = sb("a1024", [128, 8], F32)
        a1536 = sb("a1536", [128, 8], F32)
        p2048 = sb("p2048", [128, 8], F32)
        p4096 = sb("p4096", [128, 8], F32)
        ptmp = sb("ptmp", [128, 8], F32)
        ptmp2 = sb("ptmp2", [128, 8], F32)
        carry = sb("carry", [128, 8], F32)
        carry2 = sb("carry2", [128, 8], F32)
        coef = sb("coef", [128, 64], F32)
        crep = sb("crep", [128, 64], F32)
        m1_sb = sb("m1_sb", [128, 64], F32)
        mp_sb = sb("mp_sb", [128, 64], F32)
        mp2_sb = sb("mp2_sb", [128, 64], F32)
        gg_sb = sb("gg_sb", [128, 64], F32)
        u_sb = sb("u_sb", [128, 64], F32)
        stA = sb("stA", [128, 32], F32)
        stC = sb("stC", [128, 32], F32)
        no_sb = sb("no_sb", [128, 33], BF16)
        po_sb = sb("po_sb", [128, 33], BF16)
        bco_sb = sb("bco_sb", [1, 128], BF16)
        eps_sb = sb("eps_sb", [1, 1], F32)
        neg1_sb = sb("neg1_sb", [128, 1], F32)
        mu2 = sb("mu2", [1, TQ], F32)
        rstd = sb("rstd", [1, TQ], BF16)
        nmr = sb("nmr", [1, TQ], BF16)
        xbf = sb("xbf", [128, CH * TQ], BF16)
        sqs = sb("sqs", [128, 2 * TQ], BF16)
        y1bf = sb("y1bf", [128, CH * TQ], BF16)
        t1bf = sb("t1bf", [128, TQ], BF16)
        t2bf = sb("t2bf", [128, 4 * TQ], BF16)
        ybf = sb("ybf", [128, CH * TQ], BF16)
        gbf = sb("gbf", [128, 2 * TQ], BF16)
        tpbf = sb("tpbf", [128, TQ], BF16)
        x2st = sb("x2st", [128, 2 * TQ], F32)
        outt = sb("outt", [128, 2 * TQ], F32)

        ps_a = psb("ps_a", [128, 512])
        ps_b = psb("ps_b", [128, 512])
        ps_e = psb("ps_e", [128, 512])
        ps_f = psb("ps_f", [128, 512])
        ps_c = psb("ps_c", [128, 512])
        ps_d = psb("ps_d", [128, 512])
        ps_st = psb("ps_st", [33, 512])
        ps_z = psb("ps_z", [32, 512])
        ps_p4 = [ps_a, ps_b, ps_e, ps_f]

        s_ld = sem("s_ld")
        s_lg = sem("s_lg")
        s_lx = [sem("s_lx0"), sem("s_lx1")]
        s_q = sem("s_q")
        s_wl = [sem("s_wl0"), sem("s_wl1"), sem("s_wl2"), sem("s_wl3")]
        s_lx2 = [sem("s_lx20"), sem("s_lx21")]
        s_wo = [sem("s_wo0"), sem("s_wo1")]
        s_wol = sem("s_wol")
        s_od = [sem("s_od0"), sem("s_od1")]
        s_qc = sem("s_qc")
        s_es2 = [sem("s_es20"), sem("s_es21")]
        s_p2 = sem("s_p2")
        s_mm = sem("s_mm")
        s_pfb = [sem("s_pfb0"), sem("s_pfb1"), sem("s_pfb2"), sem("s_pfb3")]
        s_sq2 = sem("s_sq2")
        s_stp = sem("s_stp")
        s_r1 = sem("s_r1")
        s_r2 = sem("s_r2")
        s_r3 = sem("s_r3")
        s_r4 = sem("s_r4")
        s_r5 = sem("s_r5")
        s_bc = sem("s_bc")
        s_bce = sem("s_bce")
        s_apl = sem("s_apl")
        s_gel = sem("s_gel")
        s_ar = [sem("s_ar0"), sem("s_ar1")]
        s_bpr = sem("s_bpr")
        s_scn = sem("s_scn")
        s_scn2 = sem("s_scn2")
        s_vch = sem("s_vch")
        s_vch2 = sem("s_vch2")
        s_cc = sem("s_cc")
        s_cc2 = sem("s_cc2")
        s_g = sem("s_g")
        s_csc = sem("s_csc")
        s_sil = sem("s_sil")
        s_pou = sem("s_pou")
        s_woc = sem("s_woc")
        s_zm = sem("s_zm")
        s_ze = sem("s_ze")

        def a_bcast(c, n):
            return bass.AP(A_sb.ap().tensor, c, [[8, 128], [0, n]])

        def lbT_slice(w, mc):
            return bass.AP(lbt_sb.ap().tensor, w * D + mc * 128,
                           [[NW * D, 32], [1, 128]])

        def y1t(inst, c):
            if _SECOND.get(inst):
                t = qcbf if c < 4 else esbf
                return t[:, (c % 4) * TQ:(c % 4) * TQ + TQ]
            return y1bf[:, c * TQ:(c + 1) * TQ]

        with nc.Block() as blk:
            @blk.sync
            def _(sync):
                nld = [0]

                def d(out, in_):
                    sync.dma_start(out=out, in_=in_).then_inc(s_ld, 16)
                    nld[0] += 16
                d(bias_sb[:], bass.AP(bias_h, 0, [[8, 128], [1024, NW], [1, 8]]))
                d(lng_sb[:], bass.AP(lng_h, 0, [[8, 128], [1024, 3], [1, 8]]))
                d(lnb_sb[:], bass.AP(lnb_h, 0, [[8, 128], [1024, 3], [1, 8]]))
                d(va_sb[:], va_h.ap())
                d(vb_sb[:], vb_h.ap())
                d(vbin_sb[:], vbin_h.ap())
                d(vbout_sb[:], vbout_h.ap())
                d(m1_sb[:], m1_h.ap())
                d(mp_sb[:], mp_h.ap())
                d(mp2_sb[:], mp2_h.ap())
                d(no_sb[:], no33_c.ap())
                d(po_sb[:], po33_c.ap())
                d(bco_sb[:], bcones.ap())

                def arena_stream(w, mc):
                    cnt = _ar_cnt(w, mc)
                    if cnt >= 2:
                        sync.wait_ge(s_ar[mc % 2], 16 * (cnt - 1))
                    sync.dma_start(
                        out=xnbf[:, mc * 2048 + (w % 2) * 1024:
                                 mc * 2048 + (w % 2) * 1024 + 1024],
                        in_=bass.AP(w_hbm, w * 1048576 + mc * 131072,
                                    [[1024, 128], [1, 1024]]),
                    ).then_inc(s_ar[mc % 2], 16)

                for u in UNITS:
                    if u[0] == "W":
                        w = u[1]
                        if w == 3:
                            pos, bank, seq = _GINFO[("p", 1, 0, 1, 7)]
                            sync.wait_ge(s_pfb[bank], seq + 1)
                            sync.wait_ge(s_wo[0], 16 * 4 * 3)
                            sync.wait_ge(s_wo[1], 16 * 4 * 3)
                            for mc in range(CH):
                                arena_stream(2, mc)
                        if w == 5:
                            pos, bank, seq = _GINFO[("p", 3, 0, 1, 7)]
                            sync.wait_ge(s_pfb[bank], seq + 1)
                            sync.wait_ge(s_wo[0], 16 * 4 * 4)
                            sync.wait_ge(s_wo[1], 16 * 4 * 4)
                            for mc in range(CH):
                                arena_stream(3, mc)
                    elif u[0] == "B":
                        _, it, b = u
                        if b == 0:
                            for p in range(2):
                                w = p
                                for mc in range(CH):
                                    j = _stream_j0(it, p, mc)
                                    sync.wait_ge(s_wo[0], 16 * 4 * (w + 1))
                                    sync.wait_ge(s_wo[1], 16 * 4 * (w + 1))
                                    if j >= 4:
                                        sync.wait_ge(
                                            s_mm,
                                            _GINFO[_stream_key0(j - 4)][0] + 1)
                                    sync.dma_start(
                                        out=wstr[:, (j % 4) * 1024:
                                                 (j % 4) * 1024 + 1024],
                                        in_=bass.AP(w_hbm,
                                                    w * 1048576 + mc * 131072,
                                                    [[1024, 128], [1, 1024]]),
                                    ).then_inc(s_wl[j % 4], 16)
                    elif u[0] == "P" and u[1:] == (2, 3, 1):
                        if True:
                            for mc in range(CH):
                                sync.wait_ge(s_wo[0], 16 * 4 * 5)
                                sync.wait_ge(s_wo[1], 16 * 4 * 5)
                                sync.wait_ge(s_mm, _GINFO[("p", 3, 1, 0, mc)][0] + 1)
                                arena_stream(4, mc)
                            for mc in range(CH):
                                sync.wait_ge(s_wo[0], 16 * 4 * 6)
                                sync.wait_ge(s_wo[1], 16 * 4 * 6)
                                sync.wait_ge(s_mm, _GINFO[("p", 3, 1, 1, mc)][0] + 1)
                                arena_stream(5, mc)
                sync.wait_ge(s_od[0], 16 * 16)
                sync.wait_ge(s_od[1], 16 * 16)

            @blk.scalar
            def _(scalar):
                def rows_a(inst):
                    scalar.wait_ge(s_stp, 8 * (inst + 1))
                    scalar.activation(mu2[:], ps_st[0:1, :],
                                      AF.Square).then_inc(s_r2, 1)
                    scalar.wait_ge(s_r3, inst + 1)
                    scalar.activation(mu2[:], nmr[:], AF.Ln, bias=eps_sb[0:1, 0:1])
                    scalar.activation(rstd[:], mu2[:], AF.Exp,
                                      scale=-0.5).then_inc(s_r4, 1)

                def rows_s(inst):
                    rows_a(inst)

                def silus(it):
                    for mc in range(CH):
                        pos2, bank2, seq2 = _GINFO[("p", it, 2, 1, mc)]
                        k = it * 8 + mc
                        scalar.wait_ge(s_pfb[bank2], seq2 + 1)
                        if k >= 2:
                            scalar.wait_ge(s_pou, k - 1)
                        scalar.dma_start(
                            out=x2st[:, (k % 2) * TQ:(k % 2) * TQ + TQ],
                            in_=bass.AP(x_fm, mc * 128 * TOK + it * TQ,
                                        [[TOK, 128], [1, TQ]]),
                        ).then_inc(s_lx2[k % 2], 16)
                        if k >= 2:
                            kp = k - 2
                            scalar.dma_start(
                                out=bass.AP(out_fm,
                                            (kp % 8) * 128 * TOK + (kp // 8) * TQ,
                                            [[TOK, 128], [1, TQ]]),
                                in_=outt[:, (kp % 2) * TQ:(kp % 2) * TQ + TQ],
                            ).then_inc(s_od[kp % 2], 16)
                        scalar.activation(gbf[:, (mc % 2) * TQ:(mc % 2) * TQ + TQ],
                                          Sst[:, it * 4096 + mc * TQ:
                                              it * 4096 + mc * TQ + TQ],
                                          AF.Silu).then_inc(s_sil, 1)

                scalar.wait_ge(s_ld, 13 * 16)
                scalar.activation(A_sb[:], va_sb[:], AF.Sigmoid).then_inc(s_wol, 1)
                bufb_open = [False]
                for u in UNITS:
                    if u[0] == "W":
                        continue
                    elif u[0] == "A":
                        tq = u[1]
                        inst = _inst_a(tq)
                        rows_s(inst)
                        if tq == 0:
                            scalar.wait_ge(s_bpr, 1)
                        for mc in range(CH):
                            pos, bank, seq = _GINFO[("h", tq, mc)]
                            scalar.wait_ge(s_mm, pos + 1)
                            scalar.activation(
                                bhbf[:, mc * TOK + tq * TQ: mc * TOK + tq * TQ + TQ],
                                ps_p4[bank][:], AF.Identity,
                                bias=bprod_sb[:, mc:mc + 1],
                                scale=vb_sb[:, mc:mc + 1],
                            ).then_inc(s_pfb[bank], 1)
                    elif u[0] == "B" or u[0] == "P":
                        def zev(it, b, p):
                            zi = _ZINFO[(it, b, p)]
                            scalar.wait_ge(s_zm, zi + 1)
                            if zi >= 1:
                                pk = _ZREV[zi - 1]
                                scalar.wait_ge(
                                    s_mm,
                                    _GINFO[("p", pk[0], pk[1], pk[2], 7)][0] + 1)
                            scalar.activation(zbf[:], ps_z[:],
                                              AF.Copy).then_inc(s_ze, 1)

                        def evacs_rows(it, b, inst):
                            if _SECOND.get(inst) and not bufb_open[0]:
                                scalar.wait_ge(s_p2, 96)
                                bufb_open[0] = True
                            for mc in range(CH):
                                pos, bank, seq = _GINFO[("p", it, b, 0, mc)]
                                scalar.wait_ge(s_mm, pos + 1)
                                scalar.activation(
                                    y1t(inst, mc),
                                    ps_p4[bank][:], AF.Identity,
                                    bias=bias_sb[:, 2 * b * 8 + mc:
                                                 2 * b * 8 + mc + 1],
                                ).then_inc(s_pfb[bank], 1)
                            rows_a(inst)

                        def gelus(it, b, inst):
                            for mc in range(CH):
                                k = inst * 8 + mc
                                scalar.wait_ge(s_apl, k + 1)
                                scalar.wait_ge(s_stp, k + 1)
                                scalar.activation(
                                    y1t(inst, mc),
                                    t2bf[:, (mc % 4) * TQ:(mc % 4) * TQ + TQ],
                                    AF.Gelu,
                                    scale=lng_sb[:, b * 8 + mc:b * 8 + mc + 1],
                                    bias=lnb_sb[:, b * 8 + mc:b * 8 + mc + 1],
                                ).then_inc(s_gel, 1)

                        if u[0] == "B":
                            _, it, b = u
                            inst = _inst_b(it, b)
                            zev(it, b, 0)
                            evacs_rows(it, b, inst)
                            gelus(it, b, inst)
                            zev(it, b, 1)
                            if b == 2:
                                silus(it)
                        else:
                            _, itX, itY, b = u
                            iX, iY = _inst_b(itX, b), _inst_b(itY, b)
                            zev(itX, b, 0)
                            evacs_rows(itX, b, iX)
                            zev(itY, b, 0)
                            gelus(itX, b, iX)
                            evacs_rows(itY, b, iY)
                            zev(itX, b, 1)
                            gelus(itY, b, iY)
                            zev(itY, b, 1)
                            if b == 2:
                                silus(itX)
                                silus(itY)
                                if itY == 3:
                                    for kp in (30, 31):
                                        scalar.wait_ge(s_pou, kp + 1)
                                        scalar.dma_start(
                                            out=bass.AP(out_fm,
                                                        (kp % 8) * 128 * TOK
                                                        + (kp // 8) * TQ,
                                                        [[TOK, 128], [1, TQ]]),
                                            in_=outt[:, (kp % 2) * TQ:
                                                     (kp % 2) * TQ + TQ],
                                        ).then_inc(s_od[kp % 2], 16)

            @blk.gpsimd
            def _(gpsimd):
                gp_units = [("A", 0), ("A", 1), ("W", 0), ("W", 1),
                            ("AL", 2), ("AL", 3), ("W", 2), ("W", 3),
                            ("W", 4), ("W", 5), ("AX", 3)]
                qsem = [s_q, s_qc]

                def qload(w, kc):
                    g = w * 8 + kc
                    gpsimd.dma_start(
                        out=qbf[:, (kc % 2) * 1024:(kc % 2) * 1024 + 1024],
                        in_=bass.AP(q_t, w * D * D + kc * 128 * D,
                                    [[D, 128], [1, D]]),
                    ).then_inc(qsem[g % 2], 16)

                def esload(w, kc):
                    g = w * 8 + kc
                    gpsimd.dma_start(
                        out=esbf[:, (g % 2) * 1024:(g % 2) * 1024 + 1024],
                        in_=bass.AP(sc_t, w * 64 * D + kc * 8 * D,
                                    [[D, 8], [0, 16], [1, D]]),
                    ).then_inc(s_es2[g % 2], 16)

                for u in gp_units:
                    if u[0] == "W":
                        w = u[1]
                        if w == 0:
                            gpsimd.dma_start(
                                out=wio[:],
                                in_=bass.AP(wint_h, 0,
                                            [[D, 128], [128 * D, CH], [1, D]]),
                            ).then_inc(s_woc, 16)
                        qload(w, 0)
                        qload(w, 1)
                        esload(w, 0)
                        for kc in range(CH):
                            i = w * 8 + kc
                            hfq = (i % 2) * 1024
                            gpsimd.wait_ge(qsem[i % 2], 16 * (i // 2 + 1))
                            gpsimd.tensor_scalar(
                                qcbf[:, hfq:hfq + 1024],
                                qbf[:, hfq:hfq + 1024],
                                1.0 / 7.5, -1.0, OP.mult, OP.add)
                            if kc + 2 < CH:
                                qload(w, kc + 2)
                            if kc + 1 < CH:
                                esload(w, kc + 1)
                            gpsimd.wait_ge(s_es2[i % 2], 16 * (i // 2 + 1))
                            if i >= 2:
                                gpsimd.wait_ge(s_wo[i % 2], 16 * (i // 2))
                            gpsimd.tensor_mul(
                                wchk[:, hfq:hfq + 1024],
                                qcbf[:, hfq:hfq + 1024],
                                esbf[:, hfq:hfq + 1024],
                            ).then_inc(s_p2, 2)
                            gpsimd.dma_start(
                                out=bass.AP(w_hbm, w * 1048576 + kc * 128,
                                            [[1024, 128], [131072, 8], [1, 128]]),
                                in_=wchk[:, hfq:hfq + 1024],
                            ).then_inc(s_wo[i % 2], 16)
                    elif u[0] in ("A", "AL"):
                        tq = u[1]
                        xdst = xbf if tq % 2 == 0 else ybf
                        for p in range(4):
                            kp = tq * 4 + p
                            if tq >= 2:
                                gpsimd.wait_ge(s_apl,
                                               8 * _inst_a(tq - 2) + 2 * (p + 1))
                            if kp >= 2:
                                gpsimd.wait_ge(s_lx[kp % 2], 16 * (kp // 2))
                            gpsimd.dma_start(
                                out=xdst[:, 2 * p * TQ:2 * p * TQ + 2 * TQ],
                                in_=bass.AP(x_fm, 2 * p * 128 * TOK + tq * TQ,
                                            [[TOK, 128], [128 * TOK, 2], [1, TQ]]),
                            ).then_inc(s_lx[kp % 2], 16)
                        if tq == 1:
                            for w in range(NW):
                                gpsimd.dma_start(
                                    out=latall[:, w * 256:(w + 1) * 256],
                                    in_=bass.AP(lat_h, w * D * 32,
                                                [[32, 128], [128 * 32, 8],
                                                 [1, 32]]),
                                ).then_inc(s_lg, 16)
                            gpsimd.dma_start(out=lbt_sb[:],
                                             in_=lbt_h.ap()).then_inc(s_lg, 16)
                    elif u[0] == "AX":
                        if True:
                            gpsimd.wait_ge(s_mm, _GINFO[("h", 3, 7)][0] + 1)
                            gpsimd.dma_start(
                                out=wio[:],
                                in_=bass.AP(woutt_h, 0,
                                            [[D, 128], [128 * D, CH], [1, D]]),
                            ).then_inc(s_woc, 16)
                            gpsimd.wait_ge(s_scn2, 32)
                            gpsimd.dma_start(out=f_bnc.ap(),
                                             in_=stA[:, 24:32]).then_inc(s_cc, 16)
                            gpsimd.wait_ge(s_cc, 16)
                            gpsimd.collective_compute(
                                "AllGather", OP.bypass,
                                replica_groups=[list(range(NCORE))],
                                ins=[f_bnc.ap().opt()],
                                outs=[g_bnc.ap().opt()],
                            ).then_inc(s_cc2, 1)
                            gpsimd.wait_ge(s_cc2, 1)
                            gpsimd.dma_start(
                                out=gg_sb[:],
                                in_=bass.AP(g_bnc, 0, [[8, 128], [1024, 8], [1, 8]]),
                            ).then_inc(s_g, 16)

            @blk.tensor
            def _(tensor):
                tensor.wait_ge(s_ld, 13 * 16)

                def proj_group(key, lhsT_tile, rhs_tiles, lora_w=None):
                    pos, bank, seq = _GINFO[key]
                    if seq >= 1:
                        tensor.wait_ge(s_pfb[bank], seq)
                    last = 7 if lora_w is None else 8
                    for kc in range(CH):
                        mm = tensor.matmul(
                            ps_p4[bank][:], lhsT_tile(kc), rhs_tiles(kc),
                            start=(kc == 0), stop=(kc == last))
                    if lora_w is not None:
                        w, mc = lora_w
                        mm = tensor.matmul(ps_p4[bank][:], lbT_slice(w, mc), zbf[:],
                                           start=False, stop=True)
                    mm.then_inc(s_mm, 1)

                def zmms(it, b, p, rhs_tiles):
                    zi = _ZINFO[(it, b, p)]
                    w = 2 * b + p
                    if zi >= 1:
                        tensor.wait_ge(s_ze, zi)
                    for kc in range(CH):
                        mm = tensor.matmul(
                            ps_z[:],
                            latall[:, w * 256 + kc * 32: w * 256 + kc * 32 + 32],
                            rhs_tiles(kc),
                            start=(kc == 0), stop=(kc == 7))
                    mm.then_inc(s_zm, 1)

                def stats_pair(inst, c, src_bf):
                    k = inst * 8 + c
                    tensor.wait_ge(s_sq2, k + 1)
                    tensor.matmul(ps_st[:], no_sb[:], src_bf,
                                  start=(c == 0), stop=False)
                    mm = tensor.matmul(ps_st[:], po_sb[:],
                                       sqs[:, (c % 2) * TQ:(c % 2) * TQ + TQ],
                                       start=False, stop=(c == 7))
                    mm.then_inc(s_stp, 1)

                def bcast_inst(inst):
                    tensor.wait_ge(s_r4, inst + 1)
                    tensor.wait_ge(s_r5, inst + 1)
                    if inst >= 1:
                        tensor.wait_ge(s_apl, 8 * inst)
                    tensor.matmul(ps_c[:], bco_sb[:], rstd[:], start=True, stop=True)
                    tensor.matmul(ps_d[:], bco_sb[:], nmr[:],
                                  start=True, stop=True).then_inc(s_bc, 1)

                def sproj(it):
                    tensor.wait_ge(s_woc, 32)
                    tensor.wait_ge(s_csc, 8 * (it + 1))
                    for mc in range(CH):
                        proj_group(
                            ("s", it, mc),
                            lambda kc, mc=mc: wio[:, kc * D + mc * 128:
                                                  kc * D + mc * 128 + 128],
                            lambda kc: ybf[:, kc * TQ:(kc + 1) * TQ])

                def wtile(it, b, p, mc):
                    w = 2 * b + p
                    if b == 0:
                        j = _stream_j0(it, p, mc)
                        tensor.wait_ge(s_wl[j % 4], 16 * (j // 4 + 1))
                        return lambda kc, j=j: wstr[
                            :, (j % 4) * 1024 + kc * 128:
                            (j % 4) * 1024 + kc * 128 + 128]
                    tensor.wait_ge(s_ar[mc % 2], 16 * (_ar_cnt(w, mc) + 1))
                    base = mc * 2048 + (w % 2) * 1024
                    return lambda kc, base=base: xnbf[
                        :, base + kc * 128: base + kc * 128 + 128]

                def t_first(it, b, inst, do_bcast=True):
                    if b == 0:
                        rhs = lambda kc: xnbf[:, kc * TOK + it * TQ:
                                              kc * TOK + it * TQ + TQ]
                    else:
                        need = {}
                        for mc in range(CH):
                            pos, bank, seq = _GINFO[("p", it, b - 1, 1, mc)]
                            need[bank] = max(need.get(bank, -1), seq)
                        for bank in sorted(need):
                            tensor.wait_ge(s_pfb[bank], need[bank] + 1)
                        rhs = lambda kc: Sst[:, it * 4096 + kc * TQ:
                                             it * 4096 + kc * TQ + TQ]
                    tensor.wait_ge(s_r5, inst)
                    if (it, b) == (0, 0):
                        tensor.wait_ge(s_lg, 112)
                    zmms(it, b, 0, rhs)
                    zi1 = _ZINFO[(it, b, 0)]
                    for mc in range(CH):
                        lt = wtile(it, b, 0, mc)
                        if mc == 0:
                            tensor.wait_ge(s_ze, zi1 + 1)
                        proj_group(("p", it, b, 0, mc), lt, rhs,
                                   lora_w=(2 * b, mc))
                        if mc >= 2:
                            stats_pair(inst, mc - 2, y1t(inst, mc - 2))
                    for c in (6, 7):
                        stats_pair(inst, c, y1t(inst, c))
                    if do_bcast:
                        bcast_inst(inst)

                def t_second(it, b, inst):
                    tensor.wait_ge(s_gel, 8 * (_GPRE[inst] + 1))
                    zmms(it, b, 1, lambda kc: y1t(inst, kc))
                    zi2 = _ZINFO[(it, b, 1)]
                    for mc in range(CH):
                        lt = wtile(it, b, 1, mc)
                        if mc == 0:
                            tensor.wait_ge(s_ze, zi2 + 1)
                        proj_group(("p", it, b, 1, mc), lt,
                                   lambda kc: y1t(inst, kc),
                                   lora_w=(2 * b + 1, mc))

                for u in UNITS:
                    if u[0] == "W":
                        continue
                    if u[0] == "A":
                        tq = u[1]
                        xsrc = xbf if tq % 2 == 0 else ybf
                        inst = _inst_a(tq)
                        tensor.wait_ge(s_r5, inst)
                        for c in range(CH):
                            stats_pair(inst, c, xsrc[:, c * TQ:(c + 1) * TQ])
                        bcast_inst(inst)
                        tensor.wait_ge(s_apl, 8 * (inst + 1))
                        if tq == 0:
                            tensor.wait_ge(s_woc, 16)
                        for mc in range(CH):
                            proj_group(
                                ("h", tq, mc),
                                lambda kc, mc=mc: wio[:, kc * D + mc * 128:
                                                      kc * D + mc * 128 + 128],
                                lambda kc: xnbf[:, kc * TOK + tq * TQ:
                                                kc * TOK + tq * TQ + TQ])
                    elif u[0] == "B":
                        _, it, b = u
                        inst = _inst_b(it, b)
                        t_first(it, b, inst)
                        t_second(it, b, inst)
                    else:
                        _, itX, itY, b = u
                        iX, iY = _inst_b(itX, b), _inst_b(itY, b)
                        t_first(itX, b, iX)
                        t_first(itY, b, iY)
                        t_second(itX, b, iX)
                        t_second(itY, b, iY)
                        if b == 2:
                            sproj(itX)
                            sproj(itY)

            @blk.vector
            def _(vector):
                vc = [0]

                def g(instr):
                    instr.then_inc(s_vch, 1)
                    vc[0] += 1
                    vector.wait_ge(s_vch, vc[0])

                g(vector.memset(eps_sb[:], EPS))
                g(vector.memset(neg1_sb[:], -1.0))
                vector.wait_ge(s_wol, 1)
                bufs = [ptmp, ptmp2]
                g(vector.tensor_mul(ptmp[:], A_sb[:], A_sb[:]))
                cur = ptmp
                for i in range(8):
                    dst = a512 if i == 7 else bufs[(i + 1) % 2]
                    g(vector.tensor_mul(dst[:], cur[:], cur[:]))
                    cur = dst
                g(vector.tensor_mul(a1024[:], a512[:], a512[:]))
                g(vector.tensor_mul(a1536[:], a1024[:], a512[:]))
                g(vector.tensor_mul(p2048[:], a1024[:], a1024[:]))
                g(vector.tensor_mul(p4096[:], p2048[:], p2048[:]))
                vector.tensor_mul(bprod_sb[:], vb_sb[:],
                                  vbin_sb[:]).then_inc(s_bpr, 1)
                g(vector.tensor_copy(
                    crep[:], bass.AP(p2048.ap().tensor, 0, [[8, 128], [0, 8], [1, 8]])))
                g(vector.tensor_mul(u_sb[:], mp_sb[:], crep[:]))
                g(vector.tensor_add(gg_sb[:], m1_sb[:], u_sb[:]))
                g(vector.tensor_copy(
                    crep[:], bass.AP(p4096.ap().tensor, 0, [[8, 128], [0, 8], [1, 8]])))
                g(vector.tensor_mul(u_sb[:], mp2_sb[:], crep[:]))
                g(vector.tensor_add(coef[:], gg_sb[:], u_sb[:]))

                def rows_inst(inst):
                    vector.wait_ge(s_r2, inst + 1)
                    vector.tensor_sub(nmr[:], ps_st[32:33, :],
                                      mu2[:]).then_inc(s_r3, 1)
                    vector.wait_ge(s_r4, inst + 1)
                    vector.tensor_mul(nmr[:], ps_st[0:1, :],
                                      rstd[:]).then_inc(s_r5, 1)

                def apply_inst(inst, src_bf, dst, gel_guard=False):
                    vector.wait_ge(s_bc, inst + 1)
                    for c in range(CH):
                        if gel_guard:
                            g = 8 * _GPRE[inst] + c - 3
                            if g > 0:
                                vector.wait_ge(s_gel, g)
                        vector.tensor_mul(t1bf[:], src_bf(c), ps_c[:])
                        vector.tensor_add(dst(c), t1bf[:], ps_d[:]).then_inc(s_apl, 1)

                def wait_evac(key):
                    pos, bank, seq = _GINFO[key]
                    vector.wait_ge(s_pfb[bank], seq + 1)

                def pstt_out(it):
                    for mc in range(CH):
                        pos, bank, seq = _GINFO[("s", it, mc)]
                        k = it * 8 + mc
                        vector.wait_ge(s_mm, pos + 1)
                        vector.wait_ge(s_sil, k + 1)
                        vector.scalar_tensor_tensor(
                            tpbf[:], ps_p4[bank][:],
                            vbout_sb[:, mc:mc + 1],
                            gbf[:, (mc % 2) * TQ:(mc % 2) * TQ + TQ],
                            OP.add, OP.mult,
                        ).then_inc(s_pfb[bank], 1)
                        vector.wait_ge(s_lx2[k % 2], 16 * (k // 2 + 1))
                        if k >= 2:
                            vector.wait_ge(s_od[k % 2], 16 * (k // 2))
                        vector.tensor_add(outt[:, (k % 2) * TQ:(k % 2) * TQ + TQ],
                                          x2st[:, (k % 2) * TQ:(k % 2) * TQ + TQ],
                                          tpbf[:]).then_inc(s_pou, 1)

                def cscans(it):
                    if it > 0:
                        vector.wait_ge(s_mm, _GINFO[("s", it - 1, 7)][0] + 1)
                    for c in range(CH):
                        init = (carry[:, c:c + 1] if it == 0
                                else stC[:, it * 8 + c: it * 8 + c + 1])
                        vector.tensor_tensor_scan(
                            ybf[:, c * TQ:(c + 1) * TQ], a_bcast(c, TQ),
                            bhbf[:, c * TOK + it * TQ: c * TOK + it * TQ + TQ],
                            init, OP.mult, OP.add).then_inc(s_csc, 1)

                for u in UNITS:
                    if u[0] == "W":
                        continue
                    elif u[0] == "A":
                        tq = u[1]
                        xsrc = xbf if tq % 2 == 0 else ybf
                        inst = _inst_a(tq)
                        for c in range(CH):
                            k = inst * 8 + c
                            kp = tq * 4 + c // 2
                            vector.wait_ge(s_lx[kp % 2], 16 * (kp // 2 + 1))
                            if k >= 2:
                                vector.wait_ge(s_stp, k - 1)
                            vector.tensor_mul(sqs[:, (c % 2) * TQ:(c % 2) * TQ + TQ],
                                              xsrc[:, c * TQ:(c + 1) * TQ],
                                              xsrc[:, c * TQ:(c + 1) * TQ]).then_inc(s_sq2, 1)
                        rows_inst(inst)
                        apply_inst(
                            inst,
                            lambda c: xsrc[:, c * TQ:(c + 1) * TQ],
                            lambda c: xnbf[:, c * TOK + tq * TQ: c * TOK + tq * TQ + TQ])
                        for c in range(CH):
                            wait_evac(("h", tq, c))
                            init = 0.0 if tq == 0 else stA[:, (tq - 1) * 8 + c:
                                                           (tq - 1) * 8 + c + 1]
                            vector.tensor_tensor_scan(
                                tpbf[:], a_bcast(c, TQ),
                                bhbf[:, c * TOK + tq * TQ: c * TOK + tq * TQ + TQ],
                                init, OP.mult, OP.add).then_inc(s_scn, 1)
                            vector.wait_ge(s_scn, tq * 8 + c + 1)
                            vector.tensor_copy(stA[:, tq * 8 + c: tq * 8 + c + 1],
                                               tpbf[:, TQ - 1:TQ]).then_inc(s_scn2, 1)
                    elif u[0] == "B" or u[0] == "P":
                        def v_first(it, b, inst):
                            for mc in range(CH):
                                k = inst * 8 + mc
                                wait_evac(("p", it, b, 0, mc))
                                if k >= 2:
                                    vector.wait_ge(s_stp, k - 1)
                                vector.tensor_mul(
                                    sqs[:, (mc % 2) * TQ:(mc % 2) * TQ + TQ],
                                    y1t(inst, mc),
                                    y1t(inst, mc)).then_inc(s_sq2, 1)
                            rows_inst(inst)
                            apply_inst(inst,
                                       lambda c: y1t(inst, c),
                                       lambda c: t2bf[:, (c % 4) * TQ:
                                                      (c % 4) * TQ + TQ],
                                       gel_guard=True)
                        def v_second(it, b, inst):
                            for mc in range(CH):
                                pos, bank, seq = _GINFO[("p", it, b, 1, mc)]
                                vector.wait_ge(s_mm, pos + 1)
                                sdst = Sst[:, it * 4096 + mc * TQ:
                                           it * 4096 + mc * TQ + TQ]
                                if b == 0:
                                    res_in = xnbf[:, mc * TOK + it * TQ:
                                                  mc * TOK + it * TQ + TQ]
                                else:
                                    res_in = sdst
                                vector.scalar_tensor_tensor(
                                    sdst, ps_p4[bank][:],
                                    bias_sb[:, (2 * b + 1) * 8 + mc:
                                            (2 * b + 1) * 8 + mc + 1],
                                    res_in, OP.add, OP.add,
                                ).then_inc(s_pfb[bank], 1)

                        if u[0] == "B":
                            _, it, b = u
                            inst = _inst_b(it, b)
                            v_first(it, b, inst)
                            v_second(it, b, inst)
                        else:
                            _, itX, itY, b = u
                            iX, iY = _inst_b(itX, b), _inst_b(itY, b)
                            v_first(itX, b, iX)
                            v_first(itY, b, iY)
                            v_second(itX, b, iX)
                            v_second(itY, b, iY)
                            if b == 2:
                                pstt_out(itX)
                                if itX + 1 < 4:
                                    cscans(itX + 1)
                                pstt_out(itY)
                                if itY + 1 < 4:
                                    cscans(itY + 1)
                        if u[0] == "P" and u[1:] == (0, 1, 1):
                            vector.wait_ge(s_g, 16)
                            vcc = [0]

                            def gc(instr):
                                instr.then_inc(s_vch2, 1)
                                vcc[0] += 1
                                vector.wait_ge(s_vch2, vcc[0])

                            gc(vector.tensor_mul(u_sb[:], gg_sb[:], coef[:]))
                            cbufs = [carry, carry2]
                            gc(vector.tensor_add(carry[:], u_sb[:, 0:8],
                                                 u_sb[:, 8:16]))
                            cur = 0
                            for j in range(2, 8):
                                gc(vector.tensor_add(cbufs[(cur + 1) % 2][:],
                                                     cbufs[cur][:],
                                                     u_sb[:, j * 8:(j + 1) * 8]))
                                cur = (cur + 1) % 2
                            if cur != 0:
                                gc(vector.tensor_copy(carry[:], carry2[:]))
                            for tq2, ap in ((1, a512), (2, a1024), (3, a1536)):
                                gc(vector.tensor_mul(ptmp[:], ap[:], carry[:]))
                                gc(vector.tensor_add(stC[:, tq2 * 8:(tq2 + 1) * 8],
                                                     stA[:, (tq2 - 1) * 8:tq2 * 8],
                                                     ptmp[:]))
                            cscans(0)

    return nc


_NC_CACHE = {}


def _get_nc():
    if "nc" not in _NC_CACHE:
        _NC_CACHE["nc"] = build_nc()
    return _NC_CACHE["nc"]


def _vec128(v):
    return np.ascontiguousarray(np.asarray(v, dtype=np.float32).reshape(8, 128).T)


def kernel(L0_attn_x, L0_attn_w_in, L0_attn_b_in, L0_attn_a, L0_attn_b,
           L0_attn_w_out, L0_attn_b_out, L0_ffn_qweights, L0_ffn_scales,
           L0_ffn_biases, L0_ffn_lora_a, L0_ffn_lora_b, L0_ffn_ln_gamma,
           L0_ffn_ln_beta):
    x = np.asarray(L0_attn_x, dtype=np.float32).reshape(16384, D)
    q_t = np.ascontiguousarray(np.asarray(L0_ffn_qweights, dtype=np.int32)
                               .transpose(0, 2, 1))
    sc_t = np.ascontiguousarray(np.asarray(L0_ffn_scales, dtype=np.float32)
                                .transpose(0, 2, 1))
    la = np.asarray(L0_ffn_lora_a, dtype=np.float32)
    lat = np.ascontiguousarray(la.transpose(0, 2, 1))
    lbt = np.asarray(L0_ffn_lora_b, dtype=np.float32).transpose(0, 2, 1)
    lbt_all = np.ascontiguousarray(lbt.transpose(1, 0, 2).reshape(32, NW * D))
    biases = np.ascontiguousarray(
        np.asarray(L0_ffn_biases, dtype=np.float32).reshape(NW, 8, 128)
        .transpose(0, 2, 1))
    lng = np.ascontiguousarray(
        np.asarray(L0_ffn_ln_gamma, dtype=np.float32).reshape(3, 8, 128)
        .transpose(0, 2, 1))
    lnb = np.ascontiguousarray(
        np.asarray(L0_ffn_ln_beta, dtype=np.float32).reshape(3, 8, 128)
        .transpose(0, 2, 1))
    va = _vec128(L0_attn_a)
    vb = _vec128(L0_attn_b)
    vbin = _vec128(L0_attn_b_in)
    vbout = _vec128(L0_attn_b_out)
    wint = np.ascontiguousarray(np.asarray(L0_attn_w_in, dtype=np.float32).T)
    woutt = np.ascontiguousarray(np.asarray(L0_attn_w_out, dtype=np.float32).T)

    in_maps = []
    for c in range(NCORE):
        k = c % 4
        m1 = np.zeros((128, 64), np.float32)
        mp = np.zeros((128, 64), np.float32)
        mp2 = np.zeros((128, 64), np.float32)
        base = (c // 4) * 4
        if k >= 1:
            m1[:, (base + k - 1) * 8:(base + k) * 8] = 1.0
        if k >= 2:
            mp[:, (base + k - 2) * 8:(base + k - 1) * 8] = 1.0
        if k >= 3:
            mp2[:, (base + k - 3) * 8:(base + k - 2) * 8] = 1.0
        x_fmc = np.ascontiguousarray(x[c * TOK:(c + 1) * TOK, :].T)
        in_maps.append(dict(
            x_fm=x_fmc, q_t=q_t, sc_t=sc_t, lat_h=lat, lbt_h=lbt_all,
            bias_h=biases, lng_h=lng, lnb_h=lnb,
            va_h=va, vb_h=vb, vbin_h=vbin, vbout_h=vbout,
            wint_h=wint, woutt_h=woutt,
            m1_h=m1, mp_h=mp, mp2_h=mp2,
        ))

    nc = _get_nc()
    res = run_bass_kernel_spmd(nc, in_maps, core_ids=list(range(NCORE)))
    out = np.empty((16384, D), dtype=np.float32)
    for c in range(NCORE):
        out[c * TOK:(c + 1) * TOK, :] = res.results[c]["out_fm"].T
    return out.reshape(2, 8192, D)
